# version 15
# baseline (speedup 1.0000x reference)
import numpy as np
import concourse.bass as bass
import concourse.mybir as mybir
from concourse.alu_op_type import AluOpType as OP
from concourse.bass_utils import run_bass_kernel_spmd

F32 = mybir.dt.float32
BF16 = mybir.dt.bfloat16
I32 = mybir.dt.int32
AF = mybir.ActivationFunctionType

N = 4096
D = 1024
C = 8
TT = 512
NT = N // TT
W = 512
NW = N // W
SCOLS = N + 4
HID = 2730
HIDP = 2816
M1 = 44
M1H = 22
INNER = 512
EPS = 1e-5
INV2PI = float(1.0 / (2.0 * np.pi))
TWOPI = float(2.0 * np.pi)
PI_ = float(np.pi)

ENGS = ("sync", "gpsimd", "tensor", "vector", "scalar")


class Chan:
    def __init__(self, sem, step=1):
        self.sem = sem
        self.step = step
        self.n = 0

    def inc(self):
        self.n += self.step
        return (self.sem, self.step)


class Plan:
    def __init__(self):
        self.seq = {e: [] for e in ENGS}
        self.selfch = {}
        self.keepalive = None

    def op(self, eng, fn, waits=(), incs=(), dma=False):
        rw = []
        for w_ in waits:
            if isinstance(w_, tuple):
                rw.append((w_[0].sem, w_[1]))
            else:
                rw.append((w_.sem, w_.n))
        sc = self.selfch.get(eng)
        if eng == "vector" and fn is not None and not dma and self.keepalive is not None:
            self.vcount = getattr(self, "vcount", 0) + 1
            if self.vcount % 4 == 0:
                ka, kach = self.keepalive
                self.seq["tensor"].append((ka, [(kach.sem, kach.n - 24)], [], False))
        if sc is not None and fn is not None and not dma:
            rw.append((sc.sem, sc.n))
            incs = list(incs)
            if sc not in incs:
                incs.append(sc)
        ri = [c.inc() for c in incs]
        self.seq[eng].append((fn, rw, ri, dma))

    def emit(self, eng_name, e):
        selfsem = self.selfch[eng_name].sem.num if eng_name in self.selfch else None
        hw = {}
        for fn, waits, incs, dma in self.seq[eng_name]:
            for (sem, v) in waits:
                if v <= 0:
                    continue
                key = sem.num
                if key != selfsem and hw.get(key, -1) >= v:
                    continue
                e.wait_ge(sem, v)
                if key != selfsem:
                    hw[key] = v
            if fn is not None:
                inst = fn(e)
                if incs:
                    inst.then_inc(*incs[0])
                    for (sem, s) in incs[1:]:
                        e.maybe_drain_then_inc((sem, s))
            else:
                for (sem, s) in incs:
                    e.maybe_drain_then_inc((sem, s))


def _dt_bytes(dt):
    return {F32: 4, BF16: 2, I32: 4}[dt]


class Arena:

    def __init__(self, t, nbytes):
        self.t = t
        self.nbytes = nbytes
        self.top = 0
        self.marks = []
        self.peak = 0

    def save(self):
        self.marks.append(self.top)

    def restore(self):
        self.top = self.marks.pop()

    def alloc(self, parts, cols, dt):
        start = (self.top + 31) & ~31
        b = cols * _dt_bytes(dt)
        self.top = start + b
        self.peak = max(self.peak, self.top)
        assert self.top <= self.nbytes, f"arena overflow {self.top} > {self.nbytes}"
        v = self.t[0:parts, start // 2:(start + b) // 2]
        if dt != BF16:
            v = v.bitcast(dt)
        return v


def hcols(w, c, n=W):
    base = w * W + 1 if c < 4 else w * W
    return (base, base + n)


def build(stop_after="full"):
    nc = bass.Bass("TRN2", target_bir_lowering=False)

    x_fm = nc.dram_tensor("x_fm", [D, N], F32, kind="ExternalInput").ap()
    wqkv_d = nc.dram_tensor("ffn_wqkv", [2, D, 3 * INNER], F32, kind="ExternalInput").ap()
    wo_d = nc.dram_tensor("ffn_wo", [2, INNER, D], F32, kind="ExternalInput").ap()
    w1_d = nc.dram_tensor("ffn_w1p", [2, D, 2 * HIDP], F32, kind="ExternalInput").ap()
    w2_d = nc.dram_tensor("ffn_w2p", [2, HIDP, D], F32, kind="ExternalInput").ap()
    fcols_d = nc.dram_tensor("ffn_cols", [2, 128, M1 + C + C + C], F32, kind="ExternalInput").ap()
    qscol_d = nc.dram_tensor("ffn_qscol", [2, 128, 4], F32, kind="ExternalInput").ap()
    wquT_d = nc.dram_tensor("attn_wqkvuT", [D, 4 * D], F32, kind="ExternalInput").ap()
    woutT_d = nc.dram_tensor("attn_woutT", [D, D], F32, kind="ExternalInput").ap()
    acols_d = nc.dram_tensor("attn_cols", [128, 4 * C], F32, kind="ExternalInput").ap()
    alnb_d = nc.dram_tensor("attn_lnb", [128, C], F32, kind="ExternalInput").ap()
    bkv_d = nc.dram_tensor("attn_bkv_row", [1, 2 * D], F32, kind="ExternalInput").ap()
    theta_d = nc.dram_tensor("attn_theta_row", [1, D], F32, kind="ExternalInput").ap()
    idx_d = nc.dram_tensor("idx_row", [1, N], F32, kind="ExternalInput").ap()
    rot_d = nc.dram_tensor("rot_tabs", [4, 128, N], F32, kind="ExternalInput").ap()
    p64_d = nc.dram_tensor("p64_swap", [128, 128], F32, kind="ExternalInput").ap()
    masks_d = nc.dram_tensor("tri_masks", [2, 128, 128], F32, kind="ExternalInput").ap()
    selh_d = nc.dram_tensor("sel_h", [8, 8 * 128], F32, kind="ExternalInput").ap()
    selblk_d = nc.dram_tensor("sel_blk", [8, 128, 104], F32, kind="ExternalInput").ap()
    selrep_d = nc.dram_tensor("sel_rep", [4, 8, 128], F32, kind="ExternalInput").ap()
    ones_d = nc.dram_tensor("ones_misc", [6, 128], F32, kind="ExternalInput").ap()
    onesrow_d = nc.dram_tensor("ones_row", [1, 128], F32, kind="ExternalInput").ap()

    xbk_d = nc.dram_tensor("xbk", [D, N], F32, kind="Internal").ap()
    w1s_d = nc.dram_tensor("w1s", [2, D, 2 * HIDP], BF16, kind="Internal").ap()
    w2s_d = nc.dram_tensor("w2s", [2, HIDP, D], BF16, kind="Internal").ap()
    wqkvs_d = nc.dram_tensor("wqkvs", [2, D, 3 * INNER], BF16, kind="Internal").ap()
    wos_d = nc.dram_tensor("wos", [2, INNER, D], BF16, kind="Internal").ap()
    wquTs_d = nc.dram_tensor("wquTs", [D, 4 * D], BF16, kind="Internal").ap()
    woutTs_d = nc.dram_tensor("woutTs", [D, D], BF16, kind="Internal").ap()
    out_d = nc.dram_tensor("out_fm", [D, N], F32, kind="ExternalOutput").ap()

    plan = Plan()
    from contextlib import ExitStack
    with ExitStack() as ST:
        def sb(name, shape, dt=F32):
            return ST.enter_context(nc.sbuf_tensor(name, list(shape), dt))

        def psum(name, shape, dt=F32):
            return ST.enter_context(nc.psum_tensor(name, list(shape), dt))

        def chan(name, step=1):
            return Chan(ST.enter_context(nc.semaphore(name)), step)

        z = [sb(f"z{c}", (128, SCOLS)) for c in range(C)]
        p64 = sb("p64", (128, 128), BF16)
        masks = sb("masks", (128, 2 * 128))
        selh = sb("selh", (8, 8 * 128), BF16)
        selblk = sb("selblk", (128, 8 * 104), BF16)
        selrep = sb("selrep", (8, 4 * 128), BF16)
        onescol_bf = sb("onescol_bf", (128, 1), BF16)
        onescol_f = sb("onescol_f", (128, 1))
        epscol = sb("epscol", (128, 1))
        onesrow_bf = sb("onesrow_bf", (1, 128), BF16)
        halfpicol = sb("halfpicol", (128, 1))
        tiny30col = sb("tiny30col", (128, 1))
        onesrow_f = sb("onesrow_f", (1, 128))
        fcols = sb("fcols", (128, 2 * (M1 + 3 * C)))
        qscols = sb("qscols", (128, 8))
        acols = sb("acols", (128, 4 * C))
        alnb = sb("alnb", (128, C))
        wselblk = sb("wselblk", (128, 8 * 104), BF16)
        arena_t = sb("arena", (128, 62464 // 2), BF16)
        AR = Arena(arena_t, 62464)

        pA = psum("pA", (128, 1024))
        pB = psum("pB", (128, 1024))
        pC_ = [psum(f"pC{i}", (128, 512)) for i in range(4)]

        _bt = [chan(f"bT{i}") for i in range(6)]
        _bf = [chan(f"bF{i}") for i in range(6)]
        bT = [_bt[0], _bt[0], _bt[1], _bt[1], _bt[2], _bt[3], _bt[4], _bt[5]]
        bF = [_bf[0], _bf[0], _bf[1], _bf[1], _bf[2], _bf[3], _bf[4], _bf[5]]
        eT = chan("eT"); eV = chan("eV"); eS = chan("eS"); eG = chan("eG")
        plan.selfch = {"vector": eV, "scalar": eS, "gpsimd": eG}
        plan.keepalive = (lambda e: e.ldweights(onescol_bf[:]), eV)
        qX = chan("qX", 16)
        qW = chan("qW", 16)
        qD = [chan(f"qD{i}", 16) for i in range(8)]
        qE = [chan(f"qE{i}", 16) for i in range(11)]

        BANK = {}
        def bank_ap(i):
            if i < 2:
                return pA[:, i * 512:(i + 1) * 512]
            if i < 4:
                return pB[:, (i - 2) * 512:(i - 1) * 512]
            return pC_[i - 4][:]

        def barrier():
            snap = [(eT, eT.n), (eV, eV.n), (eS, eS.n), (eG, eG.n), (qX, qX.n), (qW, qW.n)] + [
                (q, q.n) for q in qD] + [(q, q.n) for q in qE]
            for eng, ch in (("tensor", eT), ("vector", eV), ("scalar", eS), ("gpsimd", eG)):
                plan.op(eng, None, waits=[s for s in snap if s[0] is not ch], incs=[ch])
            plan.op("sync", None, waits=[(eT, eT.n), (eV, eV.n), (eS, eS.n), (eG, eG.n)])

        def dma(eng, q, out, in_):
            plan.op(eng, lambda e, o=out, i=in_: e.dma_start(out=o, in_=i), incs=[q], dma=True)

        dma("gpsimd", qW, p64[:], p64_d)
        dma("sync", qX, masks[:].rearrange("p (m k) -> p m k", m=2, k=128), masks_d.rearrange("m p k -> p m k"))
        dma("gpsimd", qW, selh[:], selh_d)
        dma("gpsimd", qW, selblk[:].rearrange("p (c k) -> p c k", c=8, k=104), selblk_d.rearrange("c p k -> p c k"))
        dma("gpsimd", qW, selrep[:].rearrange("p (c k) -> p c k", c=4, k=128), selrep_d.rearrange("c p k -> p c k"))
        dma("gpsimd", qW, onescol_bf[:], ones_d[0:1, :].rearrange("o p -> p o"))
        dma("gpsimd", qW, onesrow_bf[:], onesrow_d)
        dma("sync", qX, onescol_f[:], ones_d[2:3, :].rearrange("o p -> p o"))
        dma("sync", qX, epscol[:], ones_d[1:2, :].rearrange("o p -> p o"))
        dma("sync", qX, halfpicol[:], ones_d[3:4, :].rearrange("o p -> p o"))
        dma("sync", qX, tiny30col[:], ones_d[4:5, :].rearrange("o p -> p o"))
        dma("sync", qX, onesrow_f[:], onesrow_d)
        dma("sync", qX, fcols[:].rearrange("p (l m) -> p l m", l=2, m=M1 + 3 * C), fcols_d.rearrange("l p m -> p l m"))
        dma("sync", qX, qscols[:].rearrange("p (l m) -> p l m", l=2, m=4), qscol_d.rearrange("l p m -> p l m"))
        dma("sync", qX, acols[:], acols_d)
        dma("sync", qX, alnb[:], alnb_d)
        n_const_w = qW.n
        n_const_x = qX.n

        def fcol(l, j):
            return fcols[:, l * (M1 + 3 * C) + j: l * (M1 + 3 * C) + j + 1]

        def b1col(l, m): return fcol(l, m)
        def b2col(l, c): return fcol(l, M1 + c)
        def lngcol(l, c): return fcol(l, M1 + C + c)
        def lnbcol(l, c): return fcol(l, M1 + 2 * C + c)
        def qscol(l, j): return qscols[:, l * 4 + j: l * 4 + j + 1]
        def acol(j, c): return acols[:, j * C + c: j * C + c + 1]

        for c in range(C):
            dma("sync", qX, z[c][:, 1:N + 1], x_fm[c * 128:(c + 1) * 128, :])
            plan.op("vector", lambda e, c=c: e.memset(z[c][:, 0:1], 0.0))
            plan.op("vector", lambda e, c=c: e.memset(z[c][:, N + 1:SCOLS], 0.0), incs=[eV])
        x_loaded_x = qX.n

        plan.op("tensor", None, waits=[(qX, qX.n), (qW, qW.n), (eV, eV.n)], incs=[eT])
        plan.op("vector", None, waits=[(qX, qX.n), (qW, qW.n)], incs=[eV])
        plan.op("scalar", None, waits=[(qX, qX.n), (qW, qW.n), (eV, eV.n)], incs=[eS])
        plan.op("gpsimd", None, waits=[(qX, qX.n), (eV, eV.n)], incs=[eG])

        AR.save()
        AR.alloc(128, (AR.nbytes - 2 * (2 * HIDP * 2) - 64) // 2, BF16)
        cstg = [AR.alloc(128, 2 * HIDP, BF16) for _ in range(2)]
        cmk = {}
        cjobs = []
        for r in range(C):
            cjobs.append((wqkv_d[0, r * 128:(r + 1) * 128, :], wqkvs_d[0, r * 128:(r + 1) * 128, :], 3 * INNER))
        for r in range(4):
            cjobs.append((wo_d[0, r * 128:(r + 1) * 128, :], wos_d[0, r * 128:(r + 1) * 128, :], D))
        for l in range(2):
            for r in range(C):
                cjobs.append((w1_d[l, r * 128:(r + 1) * 128, :], w1s_d[l, r * 128:(r + 1) * 128, :], 2 * HIDP))
        for l in range(2):
            for r in range(HIDP // 128):
                cjobs.append((w2_d[l, r * 128:(r + 1) * 128, :], w2s_d[l, r * 128:(r + 1) * 128, :], D))
        for r in range(C):
            cjobs.append((wqkv_d[1, r * 128:(r + 1) * 128, :], wqkvs_d[1, r * 128:(r + 1) * 128, :], 3 * INNER))
        for r in range(4):
            cjobs.append((wo_d[1, r * 128:(r + 1) * 128, :], wos_d[1, r * 128:(r + 1) * 128, :], D))
        for r in range(C):
            cjobs.append((wquT_d[r * 128:(r + 1) * 128, :], wquTs_d[r * 128:(r + 1) * 128, :], 4 * D))
        for r in range(C):
            cjobs.append((woutT_d[r * 128:(r + 1) * 128, :], woutTs_d[r * 128:(r + 1) * 128, :], D))
        cmark_at = {11: "ffn0", 11 + 16: "w1", 27 + 44: "mlp", 71 + 12: "ffn1", len(cjobs) - 1: "all"}
        conv_marks = {}
        for i, (srcap, dstap, cols) in enumerate(cjobs):
            sl = i % 2
            plan.op("gpsimd", lambda e, srcap=srcap, sl=sl, cols=cols: e.dma_start(
                out=cstg[sl][:, 0:cols], in_=srcap), waits=[cmk.get(sl, (eT, 0))], incs=[qD[5 + sl]], dma=True)
            plan.op("sync", lambda e, dstap=dstap, sl=sl, cols=cols: e.dma_start(
                out=dstap, in_=cstg[sl][:, 0:cols]), waits=[(qD[5 + sl], qD[5 + sl].n)],
                incs=[qE[5 + sl]], dma=True)
            cmk[sl] = (qE[5 + sl], qE[5 + sl].n)
            if i in cmark_at:
                conv_marks[cmark_at[i]] = [(qE[5], qE[5].n), (qE[6], qE[6].n)]
        AR.restore()
        conv_done = conv_marks["all"]


        def ln_stats_tile(R_, src_bf, sq_bf, n=TT, sums_bank=4, sq_bank=5):
            bs, bq = bank_ap(sums_bank), bank_ap(sq_bank)
            for c in range(C):
                plan.op("tensor", lambda e, c=c, bs=bs, n=n: e.matmul(
                    bs[0:1, 0:n], onescol_bf[:], src_bf[c][:, 0:n], start=(c == 0), stop=(c == C - 1)),
                    waits=[bF[sums_bank]] if c == 0 else (),
                    incs=[bT[sums_bank]] if c == C - 1 else ())
                plan.op("tensor", lambda e, c=c, bq=bq, n=n: e.matmul(
                    bq[0:1, 0:n], onescol_bf[:], sq_bf[c][:, 0:n], start=(c == 0), stop=(c == C - 1)),
                    waits=[bF[sq_bank]] if c == 0 else (),
                    incs=[bT[sq_bank]] if c == C - 1 else ())
            mrow = lambda k: R_[0:1, k * n: (k + 1) * n]
            plan.op("vector", lambda e: e.tensor_scalar(mrow(2), bs[0:1, 0:n], 1.0 / D, None, OP.mult),
                    waits=[bT[sums_bank]])
            plan.op("vector", lambda e: e.tensor_scalar(mrow(3), bq[0:1, 0:n], 1.0 / D, None, OP.mult),
                    waits=[bT[sq_bank]], incs=[eV])
            plan.op("scalar", lambda e: e.activation(mrow(4), mrow(2), AF.Square), waits=[eV], incs=[eS])
            plan.op("vector", lambda e: e.tensor_tensor(mrow(3), mrow(3), mrow(4), OP.subtract), waits=[eS], incs=[eV])
            plan.op("scalar", lambda e: e.activation(mrow(3), mrow(3), AF.Ln, bias=epscol[0:1, :]),
                    waits=[eV], incs=[bF[sums_bank]])
            plan.op("scalar", lambda e: e.activation(mrow(0), mrow(3), AF.Exp, scale=-0.5),
                    incs=[eS, bF[sq_bank]])
            plan.op("vector", lambda e: e.tensor_tensor(mrow(1), mrow(2), mrow(0), OP.mult), waits=[eS], incs=[eV])
            return mrow(0), mrow(1)

        def replicate_rows(r_row, mr_row, n, rep_bank, hilo):
            b0, b1 = bank_ap(rep_bank), bank_ap(rep_bank + 1)
            _hi, _lo = hilo[0:1, 0:n], hilo[0:1, n:2 * n]
            plan.op("vector", lambda e, _hi=_hi: e.tensor_copy(_hi, r_row))
            plan.op("vector", lambda e, _lo=_lo, _hi=_hi: e.tensor_tensor(_lo, r_row, _hi, OP.subtract),
                    incs=[eV])
            plan.op("tensor", lambda e, _hi=_hi, b0=b0: e.matmul(b0[:, 0:n], onesrow_bf[:], _hi,
                    start=True, stop=False), waits=[bF[rep_bank], (eV, eV.n)])
            plan.op("tensor", lambda e, _lo=_lo, b0=b0: e.matmul(b0[:, 0:n], onesrow_bf[:], _lo,
                    start=False, stop=True), incs=[bT[rep_bank]])
            mk0 = (bT[rep_bank], bT[rep_bank].n)
            plan.op("vector", lambda e, _hi=_hi: e.tensor_copy(_hi, mr_row), waits=[mk0])
            plan.op("vector", lambda e, _lo=_lo, _hi=_hi: e.tensor_tensor(_lo, mr_row, _hi, OP.subtract),
                    incs=[eV])
            plan.op("tensor", lambda e, _hi=_hi, b1=b1: e.matmul(b1[:, 0:n], onesrow_bf[:], _hi,
                    start=True, stop=False), waits=[(eV, eV.n)])
            plan.op("tensor", lambda e, _lo=_lo, b1=b1: e.matmul(b1[:, 0:n], onesrow_bf[:], _lo,
                    start=False, stop=True), incs=[bT[rep_bank]])
            return b0, b1

        def ph_ln0():
            AR.save()
            xb = [AR.alloc(128, TT, BF16) for _ in range(C)]
            xq = [AR.alloc(128, TT, BF16) for _ in range(C)]
            rows = AR.alloc(1, 5 * TT, F32)
            for t in range(NT):
                t0 = t * TT + 1
                for c in range(C):
                    plan.op("vector", lambda e, c=c, t0=t0: e.tensor_copy(xb[c][:], z[c][:, t0:t0 + TT]),
                            incs=[eV])
                    plan.op("scalar", lambda e, c=c: e.activation(xq[c][:], xb[c][:], AF.Square),
                            waits=[(eV, eV.n)], incs=[eS] if c == C - 1 else ())
                plan.op("tensor", None, waits=[(eV, eV.n), (eS, eS.n)])
                r_row, mr_row = ln_stats_tile(rows, xb, xq)
                b0, b1 = replicate_rows(r_row, mr_row, TT, 0, rows[0:1, 4 * TT:5 * TT].bitcast(BF16))
                for c in range(C):
                    plan.op("vector", lambda e, c=c, t0=t0, b0=b0: e.tensor_tensor(
                        z[c][:, t0:t0 + TT], z[c][:, t0:t0 + TT], b0[:, 0:TT], OP.mult), waits=[bT[0]])
                    plan.op("vector", lambda e, c=c, t0=t0, b1=b1: e.tensor_tensor(
                        z[c][:, t0:t0 + TT], z[c][:, t0:t0 + TT], b1[:, 0:TT], OP.subtract),
                        incs=[bF[0], eV] if c == C - 1 else ())
            AR.restore()

        def dump_stream_and_finish(src=None):
            for c in range(C):
                plan.op("sync", lambda e, c=c: e.dma_start(out=out_d[c * 128:(c + 1) * 128, :], in_=z[c][:, 1:N + 1]),
                        waits=[(eV, eV.n), (eT, eT.n), (eS, eS.n), (eG, eG.n)], incs=[qX], dma=True)
            plan.op("sync", None, waits=[(qX, qX.n)])

        def ph_ffn_p1(l):
            AR.save()
            h_bf = [AR.alloc(128, W, BF16) for _ in range(C)]
            tabs = h_bf[0:4]
            o_sb = h_bf[4:8]
            qraw0 = AR.alloc(128, W, BF16)
            sq_bf = [AR.alloc(128, W, BF16) for _ in range(2)]
            rows_q = AR.alloc(8, W, F32)
            rows_k = rows_q
            wrow = AR.alloc(8, W, F32)
            nbq_f = AR.alloc(8, W, BF16)
            nbk_f = AR.alloc(8, W, BF16)
            rnq_bf = AR.alloc(8, W, BF16)
            rnk_bf = AR.alloc(8, W, BF16)
            b8_bf = AR.alloc(8, W, BF16)
            mkc = AR.alloc(8, 8, F32)
            a_st = AR.alloc(128, W, F32)
            t1_st = a_st
            t2_st = AR.alloc(128, 640, F32)
            es2_st = AR.alloc(128, 640, F32)
            qsb = [AR.alloc(128, W, BF16) for _ in range(4)]
            ksb = [[AR.alloc(128, W, BF16) for _ in range(4)] for _ in range(2)]
            vsb = [[AR.alloc(128, 520, BF16) for _ in range(4)] for _ in range(2)]
            e_bf = [AR.alloc(128, 640, BF16) for _ in range(4)]
            brep_bf = AR.alloc(128, W, BF16)
            o_un = AR.alloc(64, W, BF16)
            rcp_f = rows_q[0:1, :]
            wkv = [AR.alloc(128, 1024, BF16) for _ in range(2)]
            wob = [AR.alloc(128, 4 * 128, BF16) for _ in range(2)]

            plan.op("vector", lambda e: e.tensor_copy(wselblk[:], selblk[:]), waits=[(eG, eG.n)])
            for c in range(C):
                scol = qscol(l, 0) if c < 4 else qscol(l, 2)
                plan.op("scalar", lambda e, scol=scol: e.activation(a_st[:, 0:1], scol, AF.Square),
                        waits=[(eV, eV.n)], incs=[eS])
                grp = 32 if c < 4 else 96
                plan.op("vector", lambda e, c=c, grp=grp: e.tensor_scalar(
                    wselblk[:, c * 104 + grp:c * 104 + grp + 8], selblk[:, c * 104 + grp:c * 104 + grp + 8],
                    a_st[:, 0:1], None, OP.mult), waits=[(eS, eS.n)], incs=[eV])
            for i in range(2):
                for ts in range(4):
                    plan.op("vector", lambda e, i=i, ts=ts: e.memset(
                        vsb[i][ts][:].rearrange("p (h e) -> p h e", h=8, e=65)[:, :, 64:65], 1.0), incs=[eV])
            stash = AR.alloc(128, 32, F32)
            for c in range(4, 8):
                plan.op("vector", lambda e, c=c: e.tensor_copy(
                    stash[:, (c - 4) * 8:(c - 4) * 8 + 8],
                    z[c][:, 0:4096].rearrange("p (t k) -> p t k", k=512)[:, :, 0]), incs=[eV])

            Wq = wqkvs_d[l]
            prev_w_mark = (eT, 0)
            prev_wo_mark = (eT, 0)
            for w in range(NW):
                pw = w % 2
                for c in range(C):
                    if c < 4:
                        c0, c1 = hcols(w, c)
                        plan.op("vector", lambda e, c=c, c0=c0, c1=c1: e.tensor_copy(h_bf[c][:], z[c][:, c0:c1]),
                                incs=[eV])
                    else:
                        plan.op("vector", lambda e, c=c, w=w: e.tensor_copy(
                            h_bf[c][:, 1:W], z[c][:, w * W + 1:w * W + W]))
                        plan.op("vector", lambda e, c=c, w=w: e.tensor_copy(
                            h_bf[c][:, 0:1], stash[:, (c - 4) * 8 + w:(c - 4) * 8 + w + 1]), incs=[eV])
                hdone = (eV, eV.n)
                vm = {}
                plan.op("sync", lambda e: e.dma_start(out=wkv[0][:, 0:512], in_=Wq[0:128, 1024:1536]),
                        waits=[prev_w_mark] + (conv_marks['ffn0'] if l == 0 and w == 0 else (conv_marks['ffn1'] if l == 1 and w == 0 else [])), incs=[qE[9]], dma=True)
                for kc in range(C):
                    if kc + 1 < C:
                        plan.op("sync", lambda e, kc=kc: e.dma_start(
                            out=wkv[(kc + 1) % 2][:, 0:512], in_=Wq[(kc + 1) * 128:(kc + 2) * 128, 1024:1536]),
                            waits=[vm.get(kc - 1, prev_w_mark)], incs=[qE[9 + (kc + 1) % 2]], dma=True)
                    wready = (qE[9 + kc % 2], qE[9 + kc % 2].n)
                    for ts in range(4):
                        plan.op("tensor", lambda e, kc=kc, ts=ts: e.matmul(
                            pC_[ts][:], h_bf[kc][:, ts * 128:(ts + 1) * 128], wkv[kc % 2][:, 0:512],
                            start=(kc == 0), stop=(kc == C - 1)),
                            waits=([wready, hdone, bF[4 + ts]] if kc == 0 else ([wready] if ts == 0 else [])),
                            incs=[bT[4 + ts]] if kc == C - 1 else ([eT] if ts == 3 else []))
                    vm[kc] = (bT[7], bT[7].n) if kc == C - 1 else (eT, eT.n)
                for ts in range(4):
                    plan.op("vector", lambda e, ts=ts, pw=pw: e.tensor_copy(
                        vsb[pw][ts][:].rearrange("p (h e) -> p h e", h=8, e=65)[:, :, 0:64],
                        pC_[ts][:].rearrange("p (h e) -> p h e", h=8, e=64)),
                        waits=[bT[4 + ts]], incs=[bF[4 + ts], eV])
                qm = {}
                plan.op("sync", lambda e: e.dma_start(out=wkv[0][:], in_=Wq[0:128, 0:1024]),
                        waits=[vm[C - 2]], incs=[qE[9]], dma=True)
                for kc in range(C):
                    if kc + 1 < C:
                        plan.op("sync", lambda e, kc=kc: e.dma_start(
                            out=wkv[(kc + 1) % 2][:], in_=Wq[(kc + 1) * 128:(kc + 2) * 128, 0:1024]),
                            waits=[qm.get(kc - 1, vm[C - 1])], incs=[qE[9 + (kc + 1) % 2]], dma=True)
                    wready = (qE[9 + kc % 2], qE[9 + kc % 2].n)
                    for m in range(8):
                        plan.op("tensor", lambda e, kc=kc, m=m: e.matmul(
                            bank_ap(m)[:], wkv[kc % 2][:, m * 128:(m + 1) * 128], h_bf[kc][:],
                            start=(kc == 0), stop=(kc == C - 1)),
                            waits=([wready, bF[m]] if kc == 0 else ([wready] if m == 0 else [])),
                            incs=[bT[m]] if kc == C - 1 else ([eT] if m == 7 else []))
                    qm[kc] = (bT[7], bT[7].n) if kc == C - 1 else (eT, eT.n)
                prev_w_mark = qm[C - 1]
                for j in range(4):
                    plan.op("gpsimd", lambda e, j=j, w=w: e.dma_start(
                        out=tabs[j][:], in_=rot_d[j, :, w * W:(w + 1) * W]),
                        waits=[qm[C - 1]] if j == 0 else [], incs=[qD[2]], dma=True)
                tabs_ok = (qD[2], qD[2].n)
                plan.op("vector", lambda e: e.tensor_copy(qraw0[:], bank_ap(0)[:]), waits=[bT[0]],
                        incs=[bF[0], eV])
                bsums = bank_ap(0)
                for m in range(8):
                    src = qraw0[:] if m == 0 else bank_ap(m)[:]
                    plan.op("scalar", lambda e, m=m, src=src: e.activation(sq_bf[m % 2][:], src, AF.Square),
                            waits=([(eV, eV.n)] if m == 0 else [bT[m]]) + [(eT, eT.n)], incs=[eS])
                    plan.op("tensor", lambda e, m=m: e.matmul(
                        bsums[0:104, :], wselblk[:, m * 104:m * 104 + 104], sq_bf[m % 2][:],
                        start=(m == 0), stop=(m == 7)),
                        waits=[(eS, eS.n)] + ([bF[0]] if m == 0 else []),
                        incs=[bT[0]] if m == 7 else [eT])
                def norm_chain(grp0, rows, rn_bf_t, nb_t):
                    plan.op("scalar", lambda e, grp0=grp0, rows=rows: e.activation(
                        rows[:], bsums[grp0:grp0 + 8, :], AF.Ln), waits=[bT[0], (eV, eV.n)], incs=[eS])
                    plan.op("scalar", lambda e, grp0=grp0: e.activation(
                        wrow[:], bsums[grp0 + 32:grp0 + 40, :], AF.Ln), incs=[eS])
                    plan.op("vector", lambda e, rows=rows: e.tensor_tensor(
                        wrow[:], wrow[:], rows[:], OP.subtract), waits=[(eS, eS.n)], incs=[eV])
                    plan.op("scalar", lambda e, rows=rows: e.activation(rows[:], rows[:], AF.Exp, scale=-0.5),
                            waits=[(eV, eV.n)], incs=[eS])
                    plan.op("scalar", lambda e, nb_t=nb_t: e.activation(nb_t[:], wrow[:], AF.Exp, scale=0.5),
                            incs=[eS])
                    plan.op("vector", lambda e, rows=rows, rn_bf_t=rn_bf_t: e.tensor_scalar(
                        rn_bf_t[:], rows[:], 1e12, None, OP.min), waits=[(eS, eS.n)], incs=[eV])
                norm_chain(0, rows_q, rnq_bf, nbq_f)
                norm_chain(64, rows_k, rnk_bf, nbk_f)
                plan.op("vector", lambda e: e.tensor_reduce(mkc[:, 0:1], nbk_f[:], mybir.AxisListType.X, OP.max))
                if w == 0:
                    plan.op("vector", lambda e: e.tensor_copy(mkc[:, 2:3], mkc[:, 0:1]))
                else:
                    plan.op("vector", lambda e: e.tensor_tensor(mkc[:, 2:3], mkc[:, 0:1], mkc[:, 1:2], OP.max))
                plan.op("vector", lambda e: e.tensor_copy(mkc[:, 1:2], mkc[:, 0:1]))
                plan.op("vector", lambda e: e.tensor_scalar(b8_bf[:], nbq_f[:], mkc[:, 2:3], 8.0,
                                                            OP.mult, OP.mult), incs=[eV])
                bsums_done = (eV, eV.n)
                for m in list(range(1, 8)) + [0]:
                    qside = m < 4
                    cpair = m % 4
                    selr = selrep[0:8, cpair * 128:cpair * 128 + 128]
                    rn_t = rnq_bf if qside else rnk_bf
                    sc = qscol(l, 0 if qside else 2)
                    ctab = tabs[0] if qside else tabs[2]
                    stab = tabs[1] if qside else tabs[3]
                    dstq = qsb[cpair] if qside else ksb[pw][cpair]
                    plan.op("tensor", lambda e, selr=selr, rn_t=rn_t: e.matmul(
                        bank_ap(0)[:], selr, rn_t[:], start=True, stop=True),
                        waits=[bF[0], bsums_done], incs=[bT[0]])
                    plan.op("vector", lambda e, sc=sc: e.tensor_scalar(a_st[:], bank_ap(0)[:], sc, None, OP.mult),
                            waits=[bT[0]], incs=[bF[0], eV])
                    qsrc = qraw0[:] if m == 0 else bank_ap(m)[:]
                    plan.op("vector", lambda e, qsrc=qsrc, dstq=dstq: e.tensor_tensor(
                        dstq[:], qsrc, a_st[:], OP.mult),
                        incs=[bF[m], eV] if m != 0 else [eV])
                    sw = m if m != 0 else 7
                    plan.op("tensor", lambda e, dstq=dstq, sw=sw: e.matmul(
                        bank_ap(sw)[:], p64[:], dstq[:], start=True, stop=True),
                        waits=[bF[sw], (eV, eV.n)], incs=[bT[sw]])
                    plan.op("vector", lambda e, dstq=dstq, ctab=ctab: e.tensor_tensor(
                        t1_st[:], dstq[:], ctab[:], OP.mult), waits=[tabs_ok])
                    plan.op("vector", lambda e, stab=stab, sw=sw: e.tensor_tensor(
                        t2_st[:, 0:W], bank_ap(sw)[:], stab[:], OP.mult), waits=[bT[sw]], incs=[bF[sw]])
                    plan.op("vector", lambda e, dstq=dstq: e.tensor_tensor(
                        dstq[:], t1_st[:], t2_st[:, 0:W], OP.add), incs=[eV])
                rot_done = (eV, eV.n)
                avm = {}
                e_ok = {}
                for h in range(8):
                    hc, hr = h // 2, (h % 2) * 64
                    plan.op("tensor", lambda e, h=h: e.matmul(
                        bank_ap(4)[:], selh[0:8, h * 128:(h + 1) * 128], b8_bf[:], start=True, stop=True),
                        waits=[bF[4], rot_done], incs=[bT[4]])
                    plan.op("vector", lambda e: e.tensor_copy(brep_bf[:], bank_ap(4)[:]),
                            waits=[bT[4]], incs=[bF[4], eV])
                    brep_ok = (eV, eV.n)
                    ou = 5 + (h % 2)
                    tcis = {t: [ci for ci in range(5) if 4 * (w - 1) + t + ci >= 0] for t in range(4)}
                    scm = {}

                    def emit_scores(t):
                        su = 0 if (t % 2 == 0) else 2
                        scp = pA if su == 0 else pB
                        cis = tcis[t]
                        for ci in cis:
                            kb = 4 * (w - 1) + t + ci
                            wp, cp = (kb // 4) % 2, kb % 4
                            plan.op("tensor", lambda e, ci=ci, scp=scp, wp=wp, cp=cp, hc=hc, hr=hr, t=t: e.matmul(
                                scp[:, ci * 128:(ci + 1) * 128],
                                ksb[wp][hc][hr:hr + 64, cp * 128:(cp + 1) * 128],
                                qsb[hc][hr:hr + 64, t * 128:(t + 1) * 128],
                                start=True, stop=True),
                                waits=[bF[su]] if ci == cis[0] else [],
                                incs=[bT[su]] if ci == cis[-1] else [])
                        scm[t] = (bT[su], bT[su].n)

                    emit_scores(0)
                    emit_scores(1)
                    for t in range(4):
                        su = 0 if (t % 2 == 0) else 2
                        scp = pA if su == 0 else pB
                        cis = tcis[t]
                        ci0, nci = cis[0], len(cis)
                        est = t2_st if t % 2 == 0 else es2_st
                        esv = est[:, 0:nci * 128].rearrange("p (c q) -> p c q", c=nci, q=128)
                        scv = scp[:, ci0 * 128:640].rearrange("p (c q) -> p c q", c=nci, q=128)
                        brv = brep_bf[:, t * 128:(t + 1) * 128].unsqueeze(1).to_broadcast((128, nci, 128))
                        plan.op("vector", lambda e, esv=esv, scv=scv, brv=brv: e.tensor_tensor(
                            esv, scv, brv, OP.subtract),
                            waits=[scm[t], brep_ok] + ([e_ok[(t + 2) % 4]] if (t + 2) % 4 in e_ok else []),
                            incs=[bF[su]])
                        plan.op("vector", lambda e, est=est, nci=nci: e.tensor_tensor(
                            est[:, (nci - 1) * 128:nci * 128], est[:, (nci - 1) * 128:nci * 128],
                            masks[:, 128:256], OP.add))
                        if ci0 == 0:
                            plan.op("vector", lambda e, est=est: e.tensor_tensor(
                                est[:, 0:128], est[:, 0:128], masks[:, 0:128], OP.add))
                        plan.op("vector", None, incs=[eV])
                        ep = e_bf[t]
                        plan.op("scalar", lambda e, ep=ep, est=est, nci=nci: e.activation(
                            ep[:, 0:nci * 128], est[:, 0:nci * 128], AF.Exp),
                            waits=[(eV, eV.n)] + ([avm[t]] if t in avm else []), incs=[eS])
                        e_ok[t] = (eS, eS.n)
                        if t + 2 < 4:
                            emit_scores(t + 2)
                    for t in range(4):
                        cis = tcis[t]
                        ci0, nci = cis[0], len(cis)
                        ep = e_bf[t]
                        for j, ci in enumerate(cis):
                            kb = 4 * (w - 1) + t + ci
                            wp, cp = (kb // 4) % 2, kb % 4
                            plan.op("tensor", lambda e, ou=ou, t=t, wp=wp, cp=cp, h=h, j=j, ci=ci, ci0=ci0, nci=nci, ep=ep: e.matmul(
                                bank_ap(ou)[0:65, t * 128:(t + 1) * 128],
                                vsb[wp][cp][:, h * 65:(h + 1) * 65],
                                ep[:, (ci - ci0) * 128:(ci - ci0 + 1) * 128],
                                start=(j == 0), stop=(j == nci - 1)),
                                waits=[e_ok[t]] + ([bF[ou]] if (t == 0 and j == 0) else []),
                                incs=([bT[ou]] if t == 3 else [eT]) if j == nci - 1 else [])
                        avm[t] = (bT[ou], bT[ou].n) if t == 3 else (eT, eT.n)
                    plan.op("scalar", lambda e, ou=ou: e.activation(
                        rcp_f[:], bank_ap(ou)[64:65, :], AF.Ln, bias=tiny30col[0:1, :]), waits=[bT[ou]])
                    plan.op("scalar", lambda e: e.activation(rcp_f[:], rcp_f[:], AF.Exp, scale=-1.0),
                            incs=[eS])
                    plan.op("tensor", lambda e: e.matmul(bank_ap(7)[0:64, :], onesrow_f[:, 0:64], rcp_f[:],
                                                         start=True, stop=True),
                            waits=[bF[7], (eS, eS.n)], incs=[bT[7]])
                    plan.op("scalar", lambda e, ou=ou: e.activation(o_un[:], bank_ap(ou)[0:64, :], AF.Copy),
                            waits=[(eV, eV.n)], incs=[eS, bF[ou]])
                    plan.op("vector", lambda e, hc=hc, hr=hr: e.tensor_tensor(
                        o_sb[hc][hr:hr + 64, :], o_un[:], bank_ap(7)[0:64, :], OP.mult),
                        waits=[bT[7], (eS, eS.n)], incs=[bF[7], eV])
                o_done = (eV, eV.n)
                womk = {}
                plan.op("sync", lambda e: e.dma_start(
                    out=wob[0][:].rearrange("p (kc m) -> p kc m", kc=4, m=128),
                    in_=wos_d[l, :, 0:128].rearrange("(kc p) m -> p kc m", p=128)),
                    waits=[prev_wo_mark] + (conv_marks['ffn0' if l == 0 else 'ffn1'] if w == 0 else []),
                    incs=[qE[3]], dma=True)
                for mc in range(C):
                    if mc + 1 < C:
                        plan.op("sync", lambda e, mc=mc: e.dma_start(
                            out=wob[(mc + 1) % 2][:].rearrange("p (kc m) -> p kc m", kc=4, m=128),
                            in_=wos_d[l, :, (mc + 1) * 128:(mc + 2) * 128].rearrange(
                                "(kc p) m -> p kc m", p=128)),
                            waits=[womk.get(mc - 1, prev_wo_mark)], incs=[qE[3 + (mc + 1) % 2]], dma=True)
                    wready = (qE[3 + mc % 2], qE[3 + mc % 2].n)
                    zu = mc % 2
                    for kc in range(4):
                        plan.op("tensor", lambda e, mc=mc, kc=kc, zu=zu: e.matmul(
                            bank_ap(zu)[:], wob[mc % 2][:, kc * 128:(kc + 1) * 128], o_sb[kc][:],
                            start=(kc == 0), stop=(kc == 3)),
                            waits=[wready, bF[zu], o_done] if kc == 0 else [],
                            incs=[bT[zu]] if kc == 3 else [])
                    womk[mc] = (bT[zu], bT[zu].n)
                    if mc == C - 1:
                        prev_wo_mark = womk[mc]
                    c0 = w * W + 1
                    plan.op("vector", lambda e, mc=mc, zu=zu, c0=c0: e.tensor_tensor(
                        z[mc][:, c0:c0 + W], z[mc][:, c0:c0 + W], bank_ap(zu)[:], OP.add),
                        waits=[bT[zu]], incs=[bF[zu], eV])
            AR.restore()

        def ph_ffn_p2(l):
            AR.save()
            hb2 = [AR.alloc(128, TT, BF16) for _ in range(C)]
            _sqt = AR.alloc(128, TT, BF16)
            sqt = [_sqt, _sqt]
            rows = AR.alloc(1, 4 * TT, F32)
            tst = AR.alloc(128, TT, F32)
            H = [AR.alloc(128, TT, BF16) for _ in range(M1H)]
            w1b = [AR.alloc(128, 2048, BF16) for _ in range(2)]
            w2b = [AR.alloc(128, M1H * 128, BF16) for _ in range(2)]
            w1mk = {}
            w2mk = {}
            W1s = w1s_d[l]
            W2s = w2s_d[l]
            stash2 = AR.alloc(128, 32, F32)
            for c in range(4, 8):
                plan.op("vector", lambda e, c=c: e.tensor_copy(
                    stash2[:, (c - 4) * 8:(c - 4) * 8 + 8],
                    z[c][:, 0:4096].rearrange("p (t k) -> p t k", k=512)[:, :, 0]), incs=[eV])
            for t in range(NT):
                for c in range(C):
                    if c < 4:
                        c0, c1 = hcols(t, c, TT)
                        plan.op("vector", lambda e, c=c, c0=c0, c1=c1: e.tensor_copy(hb2[c][:], z[c][:, c0:c1]),
                                incs=[eV])
                    else:
                        plan.op("vector", lambda e, c=c, t=t: e.tensor_copy(
                            hb2[c][:, 1:TT], z[c][:, t * TT + 1:t * TT + TT]))
                        plan.op("vector", lambda e, c=c, t=t: e.tensor_copy(
                            hb2[c][:, 0:1], stash2[:, (c - 4) * 8 + t:(c - 4) * 8 + t + 1]), incs=[eV])
                    plan.op("scalar", lambda e, c=c: e.activation(sqt[c % 2][:], hb2[c][:], AF.Square),
                            waits=[(eV, eV.n), (eT, eT.n)], incs=[eS])
                    plan.op("tensor", lambda e, c=c: e.matmul(
                        pC_[0][0:1, :], onescol_bf[:], hb2[c][:], start=(c == 0), stop=(c == C - 1)),
                        waits=[(eV, eV.n)] + ([bF[4]] if c == 0 else []),
                        incs=[bT[4]] if c == C - 1 else [])
                    plan.op("tensor", lambda e, c=c: e.matmul(
                        pC_[1][0:1, :], onescol_bf[:], sqt[c % 2][:], start=(c == 0), stop=(c == C - 1)),
                        waits=[(eS, eS.n)] + ([bF[5]] if c == 0 else []),
                        incs=[bT[5]] if c == C - 1 else [eT])
                mrow = lambda k: rows[0:1, k * TT:(k + 1) * TT] if k < 4 else tst[0:1, 0:TT]
                plan.op("vector", lambda e: e.tensor_scalar(mrow(2), pC_[0][0:1, :], 1.0 / D, None, OP.mult),
                        waits=[bT[4]])
                plan.op("vector", lambda e: e.tensor_scalar(mrow(3), pC_[1][0:1, :], 1.0 / D, None, OP.mult),
                        waits=[bT[5]], incs=[eV])
                plan.op("scalar", lambda e: e.activation(mrow(4), mrow(2), AF.Square), waits=[(eV, eV.n)], incs=[eS])
                plan.op("vector", lambda e: e.tensor_tensor(mrow(3), mrow(3), mrow(4), OP.subtract),
                        waits=[(eS, eS.n)], incs=[eV])
                plan.op("scalar", lambda e: e.activation(mrow(3), mrow(3), AF.Ln, bias=epscol[0:1, :]),
                        waits=[(eV, eV.n)], incs=[bF[4]])
                plan.op("scalar", lambda e: e.activation(mrow(0), mrow(3), AF.Exp, scale=-0.5),
                        incs=[eS, bF[5]])
                plan.op("vector", lambda e: e.tensor_tensor(mrow(1), mrow(2), mrow(0), OP.mult),
                        waits=[(eS, eS.n)], incs=[eV])
                _hl = tst[0:1, 0:TT].bitcast(BF16)
                _hi, _lo = _hl[0:1, 0:TT], _hl[0:1, TT:2 * TT]
                plan.op("vector", lambda e, _hi=_hi: e.tensor_copy(_hi, mrow(0)))
                plan.op("vector", lambda e, _lo=_lo, _hi=_hi: e.tensor_tensor(_lo, mrow(0), _hi, OP.subtract),
                        incs=[eV])
                plan.op("tensor", lambda e, _hi=_hi: e.matmul(bank_ap(2)[:], onesrow_bf[:], _hi, start=True, stop=False),
                        waits=[bF[2], (eV, eV.n)])
                plan.op("tensor", lambda e, _lo=_lo: e.matmul(bank_ap(2)[:], onesrow_bf[:], _lo, start=False, stop=True),
                        incs=[bT[2]])
                _mk0 = (bT[2], bT[2].n)
                plan.op("vector", lambda e, _hi=_hi: e.tensor_copy(_hi, mrow(1)), waits=[_mk0])
                plan.op("vector", lambda e, _lo=_lo, _hi=_hi: e.tensor_tensor(_lo, mrow(1), _hi, OP.subtract),
                        incs=[eV])
                plan.op("tensor", lambda e, _hi=_hi: e.matmul(bank_ap(3)[:], onesrow_bf[:], _hi, start=True, stop=False),
                        waits=[(eV, eV.n)])
                plan.op("tensor", lambda e, _lo=_lo: e.matmul(bank_ap(3)[:], onesrow_bf[:], _lo, start=False, stop=True),
                        incs=[bT[2]])
                for c in range(C):
                    plan.op("vector", lambda e, c=c: e.tensor_tensor(
                        tst[:], hb2[c][:], bank_ap(2)[:], OP.mult), waits=[bT[2]])
                    plan.op("vector", lambda e: e.tensor_tensor(tst[:], tst[:], bank_ap(3)[:], OP.subtract))
                    plan.op("vector", lambda e, c=c: e.tensor_scalar(
                        hb2[c][:], tst[:], lngcol(l, c), lnbcol(l, c), OP.mult, OP.add),
                        incs=([bF[2], eV] if c == C - 1 else [eV]))
                h2_ok = (eV, eV.n)
                for mp in range(M1H):
                    sg, sa = (4, 5) if mp % 2 == 0 else (6, 7)
                    sl = mp % 2
                    for hh in range(2):
                        plan.op("sync", lambda e, mp=mp, sl=sl, hh=hh: e.dma_start(
                            out=w1b[sl][:, hh * 1024:(hh + 1) * 1024].rearrange("p (kc m) -> p kc m", kc=8),
                            in_=W1s[:, hh * HIDP + mp * 128:hh * HIDP + (mp + 1) * 128].rearrange(
                                "(kc p) m -> p kc m", p=128)),
                            waits=([w1mk.get(sl, (eT, 0))] + (conv_marks['w1'] if t == 0 else [])) if hh == 0 else [],
                            incs=[qE[5 + sl]], dma=True)
                    for j, su in ((1, sg), (0, sa)):
                        for kc in range(C):
                            plan.op("tensor", lambda e, kc=kc, sl=sl, su=su, j=j: e.matmul(
                                pC_[su - 4][:], w1b[sl][:, j * 1024 + kc * 128:j * 1024 + (kc + 1) * 128],
                                hb2[kc][:], start=(kc == 0), stop=(kc == C - 1)),
                                waits=[(qE[5 + sl], qE[5 + sl].n), bF[su], h2_ok] if kc == 0 else [],
                                incs=[bT[su]] if kc == C - 1 else [])
                        w1mk[sl] = (bT[su], bT[su].n)
                    plan.op("scalar", lambda e, sg=sg, mp=mp: e.activation(
                        tst[:], pC_[sg - 4][:], AF.Gelu, bias=b1col(l, mp + M1H)),
                        waits=[bT[sg], (eV, eV.n)], incs=[eS])
                    plan.op("vector", lambda e, sa=sa, mp=mp: e.scalar_tensor_tensor(
                        H[mp][:], pC_[sa - 4][:], b1col(l, mp), tst[:], OP.add, OP.mult),
                        waits=[bT[sa], (eS, eS.n)], incs=[bF[sg], bF[sa], eV])
                H_ok = (eV, eV.n)
                for mc in range(C):
                    sl = mc % 2
                    plan.op("sync", lambda e, mc=mc, sl=sl: e.dma_start(
                        out=w2b[sl][:].rearrange("p (kc m) -> p kc m", kc=M1H, m=128),
                        in_=W2s[:, mc * 128:(mc + 1) * 128].rearrange("(kc p) m -> p kc m", p=128)),
                        waits=[w2mk.get(sl, (eT, 0))] + (conv_marks['mlp'] if t == 0 and mc < 2 else []),
                        incs=[qE[7 + sl]], dma=True)
                    zu = sl
                    for kc in range(M1H):
                        plan.op("tensor", lambda e, kc=kc, sl=sl, zu=zu: e.matmul(
                            bank_ap(zu)[:], w2b[sl][:, kc * 128:(kc + 1) * 128], H[kc][:],
                            start=(kc == 0), stop=(kc == M1H - 1)),
                            waits=[(qE[7 + sl], qE[7 + sl].n), bF[zu], H_ok] if kc == 0 else [],
                            incs=[bT[zu]] if kc == M1H - 1 else [])
                    w2mk[sl] = (bT[zu], bT[zu].n)
                    t0 = t * TT + 1
                    plan.op("vector", lambda e, mc=mc, zu=zu, t0=t0: e.scalar_tensor_tensor(
                        z[mc][:, t0:t0 + TT], bank_ap(zu)[:], b2col(l, mc), z[mc][:, t0:t0 + TT],
                        OP.add, OP.add), waits=[bT[zu]], incs=[bF[zu], eV])
            AR.restore()

        def ph_addback(src_d, to_out=False):
            AR.save()
            stg = [AR.alloc(128, TT, F32) for _ in range(4)]
            ostg = [AR.alloc(128, TT, F32) for _ in range(4)] if to_out else None
            mk = {}
            for t in range(NT):
                for c in range(C):
                    sl = (t * C + c) % 4
                    plan.op("gpsimd", lambda e, c=c, t=t, sl=sl: e.dma_start(
                        out=stg[sl][:], in_=src_d[c * 128:(c + 1) * 128, t * TT:(t + 1) * TT]),
                        waits=[mk.get(sl, (eV, 0))], incs=[qD[sl]], dma=True)
                    t0 = t * TT + 1
                    if to_out:
                        plan.op("vector", lambda e, c=c, t0=t0, sl=sl: e.scalar_tensor_tensor(
                            ostg[sl][:], z[c][:, t0:t0 + TT], 0.5, stg[sl][:], OP.mult, OP.add),
                            waits=[(qD[sl], qD[sl].n), mk.get(("o", sl), (qX, 0))], incs=[eV])
                        mk[sl] = (eV, eV.n)
                        plan.op("sync", lambda e, c=c, t=t, sl=sl: e.dma_start(
                            out=out_d[c * 128:(c + 1) * 128, t * TT:(t + 1) * TT], in_=ostg[sl][:]),
                            waits=[mk[sl]], incs=[qE[sl]], dma=True)
                        mk[("o", sl)] = (qE[sl], qE[sl].n)
                    else:
                        plan.op("vector", lambda e, c=c, t0=t0, sl=sl: e.scalar_tensor_tensor(
                            z[c][:, t0:t0 + TT], z[c][:, t0:t0 + TT], 0.5, stg[sl][:], OP.mult, OP.add),
                            waits=[(qD[sl], qD[sl].n)], incs=[eV])
                        mk[sl] = (eV, eV.n)
            if to_out:
                plan.op("sync", None, waits=[(qX, qX.n)] + [(qE[i], qE[i].n) for i in range(4)])
            AR.restore()

        def ph_attn():
            AR.save()
            theta_sb = AR.alloc(1, D, F32)
            bkv_bf = AR.alloc(1, 2 * D, BF16)
            kv_bf = AR.alloc(128, 16 * 64, BF16)
            plan.op("sync", lambda e: e.dma_start(out=theta_sb[:], in_=theta_d), incs=[qX], dma=True)
            plan.op("gpsimd", lambda e: e.dma_start(out=bkv_bf[:], in_=bkv_d), incs=[qW], dma=True)
            th_ok = [(qX, qX.n), (qW, qW.n)]
            TLP1 = 256

            AR.save()
            _blk1 = AR.alloc(128, 2048, BF16)
            xb1 = [_blk1[:, i * TLP1:(i + 1) * TLP1] for i in range(C)]
            _xq1t = AR.alloc(128, TLP1, BF16)
            xq1 = [_xq1t, _xq1t]
            rows1 = AR.alloc(1, 4 * TLP1, F32)
            tst1 = AR.alloc(128, TLP1, F32)
            y1 = [AR.alloc(128, TLP1, BF16) for _ in range(C)]
            tabs1 = [AR.alloc(128, D, BF16) for _ in range(4)]
            ust = AR.alloc(128, D, F32)
            u2st = AR.alloc(128, D, F32)
            nst = _blk1.bitcast(I32)
            wtmp1 = _blk1.bitcast(F32)
            ksil = u2st
            kp_sb = [AR.alloc(128, 2 * D, BF16) for _ in range(2)]
            _vt1 = AR.alloc(128, D, BF16)
            v_sb = [_vt1, _vt1]
            kv_acc = AR.alloc(128, 16 * 64, F32)
            wt1 = [AR.alloc(128, D, BF16) for _ in range(2)]
            idx1 = AR.alloc(1, TLP1, F32)
            wmk = {}
            idxmk = {}
            for tt in range(N // TLP1):
                t0 = tt * TLP1 + 1
                plan.op("sync", lambda e, tt=tt: e.dma_start(
                    out=idx1[:], in_=idx_d[:, tt * TLP1:(tt + 1) * TLP1]),
                    waits=[idxmk.get(0, (eT, 0))], incs=[qE[2]], dma=True)
                for c in range(C):
                    plan.op("vector", lambda e, c=c, t0=t0: e.tensor_copy(xb1[c][:], z[c][:, t0:t0 + TLP1]),
                            incs=[eV])
                    plan.op("scalar", lambda e, c=c: e.activation(xq1[c % 2][:], xb1[c][:], AF.Square),
                            waits=[(eV, eV.n), (eT, eT.n)], incs=[eS])
                    plan.op("tensor", lambda e, c=c: e.matmul(
                        pC_[0][0:1, 0:TLP1], onescol_bf[:], xb1[c][:], start=(c == 0), stop=(c == C - 1)),
                        waits=[(eV, eV.n)] + ([bF[4]] if c == 0 else []),
                        incs=[bT[4]] if c == C - 1 else [])
                    plan.op("tensor", lambda e, c=c: e.matmul(
                        pC_[1][0:1, 0:TLP1], onescol_bf[:], xq1[c % 2][:], start=(c == 0), stop=(c == C - 1)),
                        waits=[(eS, eS.n)] + ([bF[5]] if c == 0 else []),
                        incs=[bT[5]] if c == C - 1 else [eT])
                mrow1 = lambda k: rows1[0:1, k * TLP1:(k + 1) * TLP1] if k < 4 else tst1[0:1, 0:TLP1]
                plan.op("vector", lambda e: e.tensor_scalar(mrow1(2), pC_[0][0:1, 0:TLP1], 1.0 / D, None, OP.mult),
                        waits=[bT[4]])
                plan.op("vector", lambda e: e.tensor_scalar(mrow1(3), pC_[1][0:1, 0:TLP1], 1.0 / D, None, OP.mult),
                        waits=[bT[5]], incs=[eV])
                plan.op("scalar", lambda e: e.activation(mrow1(4), mrow1(2), AF.Square), waits=[(eV, eV.n)], incs=[eS])
                plan.op("vector", lambda e: e.tensor_tensor(mrow1(3), mrow1(3), mrow1(4), OP.subtract),
                        waits=[(eS, eS.n)], incs=[eV])
                plan.op("scalar", lambda e: e.activation(mrow1(3), mrow1(3), AF.Ln, bias=epscol[0:1, :]),
                        waits=[(eV, eV.n)], incs=[bF[4]])
                plan.op("scalar", lambda e: e.activation(mrow1(0), mrow1(3), AF.Exp, scale=-0.5),
                        incs=[eS, bF[5]])
                plan.op("vector", lambda e: e.tensor_tensor(mrow1(1), mrow1(2), mrow1(0), OP.mult),
                        waits=[(eS, eS.n)], incs=[eV])
                _hl = tst1[0:1, 0:TLP1].bitcast(BF16)
                _hi, _lo = _hl[0:1, 0:TLP1], _hl[0:1, TLP1:2 * TLP1]
                plan.op("vector", lambda e, _hi=_hi: e.tensor_copy(_hi, mrow1(0)))
                plan.op("vector", lambda e, _lo=_lo, _hi=_hi: e.tensor_tensor(_lo, mrow1(0), _hi, OP.subtract),
                        incs=[eV])
                plan.op("tensor", lambda e, _hi=_hi: e.matmul(pC_[0][:, 0:TLP1], onesrow_bf[:], _hi, start=True, stop=False),
                        waits=[bF[4], (eV, eV.n)])
                plan.op("tensor", lambda e, _lo=_lo: e.matmul(pC_[0][:, 0:TLP1], onesrow_bf[:], _lo, start=False, stop=True),
                        incs=[bT[4]])
                _mk0 = (bT[4], bT[4].n)
                plan.op("vector", lambda e, _hi=_hi: e.tensor_copy(_hi, mrow1(1)), waits=[_mk0])
                plan.op("vector", lambda e, _lo=_lo, _hi=_hi: e.tensor_tensor(_lo, mrow1(1), _hi, OP.subtract),
                        incs=[eV])
                plan.op("tensor", lambda e, _hi=_hi: e.matmul(pC_[1][:, 0:TLP1], onesrow_bf[:], _hi, start=True, stop=False),
                        waits=[bF[5], (eV, eV.n)])
                plan.op("tensor", lambda e, _lo=_lo: e.matmul(pC_[1][:, 0:TLP1], onesrow_bf[:], _lo, start=False, stop=True),
                        incs=[bT[5]])
                for c in range(C):
                    plan.op("vector", lambda e, c=c, t0=t0: e.tensor_tensor(
                        tst1[:], z[c][:, t0:t0 + TLP1], pC_[0][:, 0:TLP1], OP.mult), waits=[bT[4], bT[5]])
                    plan.op("vector", lambda e, c=c: e.tensor_tensor(
                        y1[c][:], tst1[:], pC_[1][:, 0:TLP1], OP.subtract),
                        incs=([bF[4], bF[5], eV] if c == C - 1 else [eV]))
                y_ok = (eV, eV.n)
                for s in range(2):
                    plan.op("tensor", lambda e, s=s: e.matmul(
                        pC_[0][:], idx1[:, s * 128:(s + 1) * 128], theta_sb[:, 0:512], start=True, stop=True),
                        waits=[(qE[2], qE[2].n), bF[4]] + (th_ok if tt == 0 and s == 0 else []),
                        incs=[bT[4]])
                    plan.op("tensor", lambda e, s=s: e.matmul(
                        pC_[1][:], idx1[:, s * 128:(s + 1) * 128], theta_sb[:, 512:1024], start=True, stop=True),
                        waits=[bF[5]], incs=[bT[5]])
                    idxmk[0] = (bT[5], bT[5].n)
                    plan.op("vector", lambda e: e.tensor_scalar(ust[:, 0:512], pC_[0][:], INV2PI, None, OP.mult),
                            waits=[bT[4], (eS, eS.n)])
                    plan.op("vector", lambda e: e.tensor_scalar(u2st[:, 0:512], pC_[0][:], INV2PI, 0.25, OP.mult, OP.add))
                    plan.op("vector", lambda e: e.tensor_scalar(ust[:, 512:1024], pC_[1][:], INV2PI, None, OP.mult),
                            waits=[bT[5]])
                    plan.op("vector", lambda e: e.tensor_scalar(u2st[:, 512:1024], pC_[1][:], INV2PI, 0.25, OP.mult, OP.add),
                            incs=[eV])
                    plan.op("scalar", lambda e: e.copy(nst[:], ust[:]), waits=[(eV, eV.n)], incs=[eS])
                    plan.op("vector", lambda e: e.scalar_tensor_tensor(
                        ust[:, 0:512], nst[:, 0:512], -TWOPI, pC_[0][:], OP.mult, OP.add), waits=[(eS, eS.n)])
                    plan.op("vector", lambda e: e.scalar_tensor_tensor(
                        ust[:, 512:1024], nst[:, 512:1024], -TWOPI, pC_[1][:], OP.mult, OP.add))
                    plan.op("vector", None, incs=[eV])
                    plan.op("vector", lambda e: e.tensor_scalar(wtmp1[:], ust[:], PI_, TWOPI, OP.is_gt, OP.mult),
                            waits=[(eV, eV.n)])
                    plan.op("vector", lambda e: e.tensor_tensor(ust[:], ust[:], wtmp1[:], OP.subtract))
                    plan.op("vector", lambda e: e.tensor_scalar(wtmp1[:], ust[:], -PI_, TWOPI, OP.is_lt, OP.mult))
                    plan.op("vector", lambda e: e.tensor_tensor(ust[:], ust[:], wtmp1[:], OP.add), incs=[eV])
                    plan.op("scalar", lambda e, s=s: e.activation(tabs1[2 * s + 1][:], ust[:], AF.Sin),
                            waits=[(eV, eV.n)], incs=[eS])
                    plan.op("scalar", lambda e: e.copy(nst[:], u2st[:]), waits=[(eV, eV.n)], incs=[eS])
                    plan.op("vector", lambda e: e.scalar_tensor_tensor(
                        u2st[:, 0:512], nst[:, 0:512], -TWOPI, pC_[0][:], OP.mult, OP.add),
                        waits=[(eS, eS.n)], incs=[bF[4]])
                    plan.op("vector", lambda e: e.scalar_tensor_tensor(
                        u2st[:, 512:1024], nst[:, 512:1024], -TWOPI, pC_[1][:], OP.mult, OP.add), incs=[bF[5]])
                    plan.op("vector", None, incs=[eV])
                    plan.op("vector", lambda e: e.tensor_scalar(wtmp1[:], u2st[:], PI_ / 2, TWOPI, OP.is_gt, OP.mult),
                            waits=[(eV, eV.n)])
                    plan.op("vector", lambda e: e.tensor_tensor(u2st[:], u2st[:], wtmp1[:], OP.subtract))
                    plan.op("vector", lambda e: e.tensor_scalar(wtmp1[:], u2st[:], -1.5 * PI_, TWOPI, OP.is_lt, OP.mult))
                    plan.op("vector", lambda e: e.tensor_tensor(u2st[:], u2st[:], wtmp1[:], OP.add), incs=[eV])
                    plan.op("scalar", lambda e, s=s: e.activation(tabs1[2 * s][:], u2st[:], AF.Sin,
                                                                 bias=halfpicol[:]),
                            waits=[(eV, eV.n)], incs=[eS])
                tabs_ok1 = (eS, eS.n)
                kmk = {}
                plan.op("sync", lambda e: e.dma_start(out=wt1[0][:], in_=wquTs_d[0:128, D:2 * D]),
                        waits=[wmk.get(0, (eT, 0))] + (conv_done if tt == 0 else []), incs=[qE[9]], dma=True)
                for kc in range(C):
                    if kc + 1 < C:
                        plan.op("sync", lambda e, kc=kc: e.dma_start(
                            out=wt1[(kc + 1) % 2][:], in_=wquTs_d[(kc + 1) * 128:(kc + 2) * 128, D:2 * D]),
                            waits=[kmk.get(kc - 1, wmk.get(1, (eT, 0)))], incs=[qE[9 + (kc + 1) % 2]], dma=True)
                    for s in range(2):
                        su = 2 * s
                        for half in range(2):
                            plan.op("tensor", lambda e, kc=kc, s=s, half=half, su=su: e.matmul(
                                bank_ap(su + half)[:], y1[kc][:, s * 128:(s + 1) * 128],
                                wt1[kc % 2][:, half * 512:(half + 1) * 512],
                                start=(kc == 0), stop=False),
                                waits=[(qE[9 + kc % 2], qE[9 + kc % 2].n)] + ([bF[su + half]] if kc == 0 else []),
                                incs=[eT] if (s == 1 and half == 1) else [])
                    kmk[kc] = (eT, eT.n)
                for s in range(2):
                    su = 2 * s
                    for half in range(2):
                        plan.op("tensor", lambda e, s=s, half=half, su=su: e.matmul(
                            bank_ap(su + half)[:], onesrow_bf[:, s * 0:128][0:1, :],
                            bkv_bf[:, half * 512:(half + 1) * 512], start=False, stop=True),
                            incs=[bT[su + half]])
                for s in range(2):
                    su = 2 * s
                    plan.op("scalar", lambda e, s=s, su=su: e.activation(
                        ksil[:, 0:512], bank_ap(su)[:], AF.Silu), waits=[bT[su], (eV, eV.n)], incs=[eS])
                    plan.op("scalar", lambda e, s=s, su=su: e.activation(
                        ksil[:, 512:1024], bank_ap(su + 1)[:], AF.Silu), waits=[bT[su + 1]],
                        incs=[eS, bF[su], bF[su + 1]])
                    kv3 = kp_sb[s][:].rearrange("p (h two d) -> p (h two) d", two=2, d=64)
                    plan.op("vector", lambda e, s=s, kv3=kv3: e.tensor_tensor(
                        kv3.rearrange("p (h two) d -> p h two d", two=2)[:, :, 0, :],
                        ksil[:].rearrange("p (h d) -> p h d", d=64),
                        tabs1[2 * s][:].rearrange("p (h d) -> p h d", d=64), OP.mult),
                        waits=[(eS, eS.n), tabs_ok1])
                    plan.op("vector", lambda e, s=s, kv3=kv3: e.tensor_tensor(
                        kv3.rearrange("p (h two) d -> p h two d", two=2)[:, :, 1, :],
                        ksil[:].rearrange("p (h d) -> p h d", d=64),
                        tabs1[2 * s + 1][:].rearrange("p (h d) -> p h d", d=64), OP.mult), incs=[eV])
                kp_ok = (eV, eV.n)
                vmk = {}
                plan.op("sync", lambda e: e.dma_start(out=wt1[0][:], in_=wquTs_d[0:128, 2 * D:3 * D]),
                        waits=[kmk[C - 1]], incs=[qE[9]], dma=True)
                for kc in range(C):
                    if kc + 1 < C:
                        plan.op("sync", lambda e, kc=kc: e.dma_start(
                            out=wt1[(kc + 1) % 2][:], in_=wquTs_d[(kc + 1) * 128:(kc + 2) * 128, 2 * D:3 * D]),
                            waits=[vmk.get(kc - 1, kmk[C - 1])], incs=[qE[9 + (kc + 1) % 2]], dma=True)
                    for s in range(2):
                        su = 2 * s
                        for half in range(2):
                            plan.op("tensor", lambda e, kc=kc, s=s, half=half, su=su: e.matmul(
                                bank_ap(su + half)[:], y1[kc][:, s * 128:(s + 1) * 128],
                                wt1[kc % 2][:, half * 512:(half + 1) * 512],
                                start=(kc == 0), stop=False),
                                waits=[(qE[9 + kc % 2], qE[9 + kc % 2].n)] + ([bF[su + half]] if kc == 0 else []),
                                incs=[eT] if (s == 1 and half == 1) else [])
                    vmk[kc] = (eT, eT.n)
                wmk[0] = vmk[C - 1]
                wmk[1] = vmk[C - 1]
                for s in range(2):
                    su = 2 * s
                    for half in range(2):
                        plan.op("tensor", lambda e, s=s, half=half, su=su: e.matmul(
                            bank_ap(su + half)[:], onesrow_bf[0:1, :],
                            bkv_bf[:, D + half * 512:D + (half + 1) * 512], start=False, stop=True),
                            incs=[bT[su + half]])
                for s in range(2):
                    su = 2 * s
                    plan.op("vector", lambda e, s=s, su=su: e.tensor_copy(v_sb[s][:, 0:512], bank_ap(su)[:]),
                            waits=[bT[su], bT[6], bT[7]], incs=[bF[su]])
                    plan.op("vector", lambda e, s=s, su=su: e.tensor_copy(v_sb[s][:, 512:1024], bank_ap(su + 1)[:]),
                            waits=[bT[su + 1]], incs=[bF[su + 1], eV])
                    v_ok = (eV, eV.n)
                    for h in range(16):
                        un = 6 + h // 8
                        plan.op("tensor", lambda e, s=s, h=h, un=un: e.matmul(
                            bank_ap(un)[:, (h % 8) * 64:(h % 8) * 64 + 64],
                            kp_sb[s][:, h * 128:(h + 1) * 128], v_sb[s][:, h * 64:(h + 1) * 64],
                            start=True, stop=True),
                            waits=([kp_ok, v_ok, bF[6], bF[7]] if h == 0 else []),
                            incs=[bT[6]] if h == 7 else ([bT[7]] if h == 15 else []))
                    for un in (6, 7):
                        if tt == 0 and s == 0:
                            plan.op("vector", lambda e, un=un: e.tensor_copy(
                                kv_acc[:, (un - 6) * 512:(un - 5) * 512], bank_ap(un)[:]),
                                waits=[bT[un]], incs=[bF[un]])
                        else:
                            plan.op("vector", lambda e, un=un: e.tensor_tensor(
                                kv_acc[:, (un - 6) * 512:(un - 5) * 512],
                                kv_acc[:, (un - 6) * 512:(un - 5) * 512], bank_ap(un)[:], OP.add),
                                waits=[bT[un]], incs=[bF[un]])
                    plan.op("vector", None, incs=[eV])
            plan.op("vector", lambda e: e.tensor_copy(kv_bf[:], kv_acc[:]), incs=[eV])
            AR.restore()
            barrier()

            AR.save()
            xb2 = [AR.alloc(128, TT, BF16) for _ in range(C)]
            _xq2t = AR.alloc(128, TT, BF16)
            xq2 = [_xq2t, _xq2t]
            rows2 = AR.alloc(1, 4 * TT, F32)
            tst2 = AR.alloc(128, TT, F32)
            y2 = [AR.alloc(128, TT, BF16) for _ in range(C)]
            qsil = [AR.alloc(128, TT, BF16) for _ in range(C)]
            ctab2 = AR.alloc(128, TT, BF16)
            stab2 = AR.alloc(128, TT, BF16)
            qful = [AR.alloc(128, TT, BF16) for _ in range(2)]
            ust2 = tst2
            u2st2 = AR.alloc(128, TT, F32)
            nst2 = AR.alloc(128, TT, I32)
            wtmp2 = nst2.bitcast(F32)
            wt2 = [AR.alloc(128, D, BF16) for _ in range(2)]
            idx2 = AR.alloc(1, TT, F32)
            wmk2 = {}
            idxmk2 = {}
            for t in range(NT):
                t0 = t * TT + 1
                plan.op("sync", lambda e, t=t: e.dma_start(
                    out=idx2[:], in_=idx_d[:, t * TT:(t + 1) * TT]),
                    waits=[idxmk2.get(0, (eT, 0))], incs=[qE[2]], dma=True)
                for c in range(C):
                    plan.op("vector", lambda e, c=c, t0=t0: e.tensor_copy(xb2[c][:], z[c][:, t0:t0 + TT]),
                            incs=[eV])
                    plan.op("scalar", lambda e, c=c: e.activation(xq2[c % 2][:], xb2[c][:], AF.Square),
                            waits=[(eV, eV.n), (eT, eT.n)], incs=[eS])
                    plan.op("tensor", lambda e, c=c: e.matmul(
                        pC_[0][0:1, :], onescol_bf[:], xb2[c][:], start=(c == 0), stop=(c == C - 1)),
                        waits=[(eV, eV.n)] + ([bF[4]] if c == 0 else []),
                        incs=[bT[4]] if c == C - 1 else [])
                    plan.op("tensor", lambda e, c=c: e.matmul(
                        pC_[1][0:1, :], onescol_bf[:], xq2[c % 2][:], start=(c == 0), stop=(c == C - 1)),
                        waits=[(eS, eS.n)] + ([bF[5]] if c == 0 else []),
                        incs=[bT[5]] if c == C - 1 else [eT])
                mrow = lambda k: rows2[0:1, k * TT:(k + 1) * TT] if k < 4 else tst2[0:1, 0:TT]
                plan.op("vector", lambda e: e.tensor_scalar(mrow(2), pC_[0][0:1, :], 1.0 / D, None, OP.mult),
                        waits=[bT[4]])
                plan.op("vector", lambda e: e.tensor_scalar(mrow(3), pC_[1][0:1, :], 1.0 / D, None, OP.mult),
                        waits=[bT[5]], incs=[eV])
                plan.op("scalar", lambda e: e.activation(mrow(4), mrow(2), AF.Square), waits=[(eV, eV.n)], incs=[eS])
                plan.op("vector", lambda e: e.tensor_tensor(mrow(3), mrow(3), mrow(4), OP.subtract),
                        waits=[(eS, eS.n)], incs=[eV])
                plan.op("scalar", lambda e: e.activation(mrow(3), mrow(3), AF.Ln, bias=epscol[0:1, :]),
                        waits=[(eV, eV.n)], incs=[bF[4]])
                plan.op("scalar", lambda e: e.activation(mrow(0), mrow(3), AF.Exp, scale=-0.5),
                        incs=[eS, bF[5]])
                plan.op("vector", lambda e: e.tensor_tensor(mrow(1), mrow(2), mrow(0), OP.mult),
                        waits=[(eS, eS.n)], incs=[eV])
                _hl = tst2[0:1, 0:TT].bitcast(BF16)
                _hi, _lo = _hl[0:1, 0:TT], _hl[0:1, TT:2 * TT]
                plan.op("vector", lambda e, _hi=_hi: e.tensor_copy(_hi, mrow(0)))
                plan.op("vector", lambda e, _lo=_lo, _hi=_hi: e.tensor_tensor(_lo, mrow(0), _hi, OP.subtract),
                        incs=[eV])
                plan.op("tensor", lambda e, _hi=_hi: e.matmul(pC_[0][:], onesrow_bf[:], _hi, start=True, stop=False),
                        waits=[bF[4], (eV, eV.n)])
                plan.op("tensor", lambda e, _lo=_lo: e.matmul(pC_[0][:], onesrow_bf[:], _lo, start=False, stop=True),
                        incs=[bT[4]])
                _mk0 = (bT[4], bT[4].n)
                plan.op("vector", lambda e, _hi=_hi: e.tensor_copy(_hi, mrow(1)), waits=[_mk0])
                plan.op("vector", lambda e, _lo=_lo, _hi=_hi: e.tensor_tensor(_lo, mrow(1), _hi, OP.subtract),
                        incs=[eV])
                plan.op("tensor", lambda e, _hi=_hi: e.matmul(pC_[1][:], onesrow_bf[:], _hi, start=True, stop=False),
                        waits=[bF[5], (eV, eV.n)])
                plan.op("tensor", lambda e, _lo=_lo: e.matmul(pC_[1][:], onesrow_bf[:], _lo, start=False, stop=True),
                        incs=[bT[5]])
                for c in range(C):
                    plan.op("vector", lambda e, c=c, t0=t0: e.tensor_tensor(
                        tst2[:], z[c][:, t0:t0 + TT], pC_[0][:], OP.mult), waits=[bT[4], bT[5]])
                    plan.op("vector", lambda e, c=c: e.tensor_tensor(
                        y2[c][:], tst2[:], pC_[1][:], OP.subtract),
                        incs=([bF[4], bF[5], eV] if c == C - 1 else [eV]))
                y_ok = (eV, eV.n)
                qmk = {}
                plan.op("sync", lambda e: e.dma_start(out=wt2[0][:], in_=wquTs_d[0:128, 0:D]),
                        waits=[wmk2.get(0, (eT, 0))], incs=[qE[9]], dma=True)
                for kc in range(C):
                    if kc + 1 < C:
                        plan.op("sync", lambda e, kc=kc: e.dma_start(
                            out=wt2[(kc + 1) % 2][:], in_=wquTs_d[(kc + 1) * 128:(kc + 2) * 128, 0:D]),
                            waits=[qmk.get(kc - 1, wmk2.get(1, (eT, 0)))], incs=[qE[9 + (kc + 1) % 2]], dma=True)
                    for m in range(8):
                        plan.op("tensor", lambda e, kc=kc, m=m: e.matmul(
                            bank_ap(m)[:], wt2[kc % 2][:, m * 128:(m + 1) * 128], y2[kc][:],
                            start=(kc == 0), stop=(kc == C - 1)),
                            waits=[(qE[9 + kc % 2], qE[9 + kc % 2].n), y_ok] + ([bF[m]] if kc == 0 else [])
                            if kc == 0 else ([(qE[9 + kc % 2], qE[9 + kc % 2].n)] if m == 0 else []),
                            incs=[bT[m]] if kc == C - 1 else ([eT] if m == 7 else []))
                    qmk[kc] = (bT[7], bT[7].n) if kc == C - 1 else (eT, eT.n)
                for m in range(8):
                    plan.op("scalar", lambda e, m=m: e.activation(
                        qsil[m][:], bank_ap(m)[:], AF.Silu, bias=acol(0, m)),
                        waits=[bT[m], (eV, eV.n)], incs=[eS, bF[m]])
                qsil_ok = (eS, eS.n)
                for hp in range(8):
                    plan.op("tensor", lambda e, hp=hp: e.matmul(
                        pC_[0][:], theta_sb[:, hp * 128:(hp + 1) * 128],
                        idx2[:], start=True, stop=True), waits=[bF[4], (qE[2], qE[2].n)],
                        incs=[bT[4]])
                    idxmk2[0] = (bT[4], bT[4].n)
                    plan.op("vector", lambda e: e.tensor_scalar(ust2[:], pC_[0][:], INV2PI, None, OP.mult),
                            waits=[bT[4], (eS, eS.n)])
                    plan.op("vector", lambda e: e.tensor_scalar(u2st2[:], pC_[0][:], INV2PI, 0.25, OP.mult, OP.add),
                            incs=[eV])
                    plan.op("scalar", lambda e: e.copy(nst2[:], ust2[:]), waits=[(eV, eV.n)], incs=[eS])
                    plan.op("vector", lambda e: e.scalar_tensor_tensor(
                        ust2[:], nst2[:], -TWOPI, pC_[0][:], OP.mult, OP.add), waits=[(eS, eS.n)])
                    plan.op("vector", None, incs=[eV])
                    plan.op("vector", lambda e: e.tensor_scalar(wtmp2[:], ust2[:], PI_, TWOPI, OP.is_gt, OP.mult),
                            waits=[(eV, eV.n)])
                    plan.op("vector", lambda e: e.tensor_tensor(ust2[:], ust2[:], wtmp2[:], OP.subtract))
                    plan.op("vector", lambda e: e.tensor_scalar(wtmp2[:], ust2[:], -PI_, TWOPI, OP.is_lt, OP.mult))
                    plan.op("vector", lambda e: e.tensor_tensor(ust2[:], ust2[:], wtmp2[:], OP.add), incs=[eV])
                    plan.op("scalar", lambda e: e.activation(stab2[:], ust2[:], AF.Sin),
                            waits=[(eV, eV.n)], incs=[eS])
                    plan.op("scalar", lambda e: e.copy(nst2[:], u2st2[:]), waits=[(eV, eV.n)], incs=[eS])
                    plan.op("vector", lambda e: e.scalar_tensor_tensor(
                        u2st2[:], nst2[:], -TWOPI, pC_[0][:], OP.mult, OP.add),
                        waits=[(eS, eS.n)], incs=[bF[4]])
                    plan.op("vector", None, incs=[eV])
                    plan.op("vector", lambda e: e.tensor_scalar(wtmp2[:], u2st2[:], PI_ / 2, TWOPI, OP.is_gt, OP.mult),
                            waits=[(eV, eV.n)])
                    plan.op("vector", lambda e: e.tensor_tensor(u2st2[:], u2st2[:], wtmp2[:], OP.subtract))
                    plan.op("vector", lambda e: e.tensor_scalar(wtmp2[:], u2st2[:], -1.5 * PI_, TWOPI, OP.is_lt, OP.mult))
                    plan.op("vector", lambda e: e.tensor_tensor(u2st2[:], u2st2[:], wtmp2[:], OP.add), incs=[eV])
                    plan.op("scalar", lambda e: e.activation(ctab2[:], u2st2[:], AF.Sin,
                                                            bias=halfpicol[:]),
                            waits=[(eV, eV.n)], incs=[eS])
                    for hh in range(2):
                        h = 2 * hp + hh
                        hr = hh * 64
                        qf = qful[hh]
                        plan.op("vector", lambda e, hr=hr, hp=hp, qf=qf: e.tensor_tensor(
                            qf[0:64, :], qsil[hp][hr:hr + 64, :], ctab2[hr:hr + 64, :], OP.mult),
                            waits=[(eS, eS.n), qsil_ok, bT[6 - 1] if False else (eT, 0)])
                        plan.op("vector", lambda e, hr=hr, hp=hp, qf=qf: e.tensor_tensor(
                            qf[64:128, :], qsil[hp][hr:hr + 64, :], stab2[hr:hr + 64, :], OP.mult),
                            incs=[eV])
                        ou = 5 + hh
                        plan.op("tensor", lambda e, h=h, qf=qf, ou=ou: e.matmul(
                            bank_ap(ou)[(h % 2) * 0:64, :] if False else bank_ap(ou)[0:64, :],
                            kv_bf[:, h * 64:(h + 1) * 64], qf[:], start=True, stop=True),
                            waits=[bF[ou], (eV, eV.n)], incs=[bT[ou]])
                        plan.op("scalar", lambda e, hp=hp, hr=hr, ou=ou: e.activation(
                            xb2[hp][hr:hr + 64, :], bank_ap(ou)[0:64, :], AF.Copy),
                            waits=[bT[ou]], incs=[eS, bF[ou]])
                o_ok = (eS, eS.n)
                for c in range(C):
                    plan.op("scalar", lambda e, c=c: e.activation(xq2[c % 2][:], xb2[c][:], AF.Square),
                            waits=[(eT, eT.n)], incs=[eS])
                    plan.op("tensor", lambda e, c=c: e.matmul(
                        pC_[0][0:1, :], onescol_bf[:], xb2[c][:], start=(c == 0), stop=(c == C - 1)),
                        waits=[o_ok] + ([bF[4]] if c == 0 else []),
                        incs=[bT[4]] if c == C - 1 else [])
                    plan.op("tensor", lambda e, c=c: e.matmul(
                        pC_[1][0:1, :], onescol_bf[:], xq2[c % 2][:], start=(c == 0), stop=(c == C - 1)),
                        waits=[(eS, eS.n)] + ([bF[5]] if c == 0 else []),
                        incs=[bT[5]] if c == C - 1 else [eT])
                plan.op("vector", lambda e: e.tensor_scalar(mrow(2), pC_[0][0:1, :], 1.0 / D, None, OP.mult),
                        waits=[bT[4]])
                plan.op("vector", lambda e: e.tensor_scalar(mrow(3), pC_[1][0:1, :], 1.0 / D, None, OP.mult),
                        waits=[bT[5]], incs=[eV])
                plan.op("scalar", lambda e: e.activation(mrow(4), mrow(2), AF.Square), waits=[(eV, eV.n)], incs=[eS])
                plan.op("vector", lambda e: e.tensor_tensor(mrow(3), mrow(3), mrow(4), OP.subtract),
                        waits=[(eS, eS.n)], incs=[eV])
                plan.op("scalar", lambda e: e.activation(mrow(3), mrow(3), AF.Ln, bias=epscol[0:1, :]),
                        waits=[(eV, eV.n)], incs=[bF[4]])
                plan.op("scalar", lambda e: e.activation(mrow(0), mrow(3), AF.Exp, scale=-0.5),
                        incs=[eS, bF[5]])
                plan.op("vector", lambda e: e.tensor_tensor(mrow(1), mrow(2), mrow(0), OP.mult),
                        waits=[(eS, eS.n)], incs=[eV])
                _hl = tst2[0:1, 0:TT].bitcast(BF16)
                _hi, _lo = _hl[0:1, 0:TT], _hl[0:1, TT:2 * TT]
                plan.op("vector", lambda e, _hi=_hi: e.tensor_copy(_hi, mrow(0)))
                plan.op("vector", lambda e, _lo=_lo, _hi=_hi: e.tensor_tensor(_lo, mrow(0), _hi, OP.subtract),
                        incs=[eV])
                plan.op("tensor", lambda e, _hi=_hi: e.matmul(pC_[0][:], onesrow_bf[:], _hi, start=True, stop=False),
                        waits=[bF[4], (eV, eV.n)])
                plan.op("tensor", lambda e, _lo=_lo: e.matmul(pC_[0][:], onesrow_bf[:], _lo, start=False, stop=True),
                        incs=[bT[4]])
                _mk0 = (bT[4], bT[4].n)
                plan.op("vector", lambda e, _hi=_hi: e.tensor_copy(_hi, mrow(1)), waits=[_mk0])
                plan.op("vector", lambda e, _lo=_lo, _hi=_hi: e.tensor_tensor(_lo, mrow(1), _hi, OP.subtract),
                        incs=[eV])
                plan.op("tensor", lambda e, _hi=_hi: e.matmul(pC_[1][:], onesrow_bf[:], _hi, start=True, stop=False),
                        waits=[bF[5], (eV, eV.n)])
                plan.op("tensor", lambda e, _lo=_lo: e.matmul(pC_[1][:], onesrow_bf[:], _lo, start=False, stop=True),
                        incs=[bT[5]])
                for c in range(C):
                    plan.op("vector", lambda e, c=c: e.tensor_tensor(
                        tst2[:], xb2[c][:], pC_[0][:], OP.mult), waits=[bT[4], bT[5]])
                    plan.op("vector", lambda e: e.tensor_tensor(tst2[:], tst2[:], pC_[1][:], OP.subtract))
                    plan.op("vector", lambda e, c=c: e.tensor_scalar(
                        xb2[c][:], tst2[:], acol(3, c), alnb[:, c:c + 1], OP.mult, OP.add),
                        incs=([bF[4], bF[5], eV] if c == C - 1 else [eV]))
                op_ok = (eV, eV.n)
                umk = {}
                plan.op("sync", lambda e: e.dma_start(out=wt2[0][:], in_=wquTs_d[0:128, 3 * D:4 * D]),
                        waits=[qmk[C - 1]], incs=[qE[9]], dma=True)
                for kc in range(C):
                    if kc + 1 < C:
                        plan.op("sync", lambda e, kc=kc: e.dma_start(
                            out=wt2[(kc + 1) % 2][:], in_=wquTs_d[(kc + 1) * 128:(kc + 2) * 128, 3 * D:4 * D]),
                            waits=[umk.get(kc - 1, qmk[C - 1])], incs=[qE[9 + (kc + 1) % 2]], dma=True)
                    for m in range(8):
                        plan.op("tensor", lambda e, kc=kc, m=m: e.matmul(
                            bank_ap(m)[:], wt2[kc % 2][:, m * 128:(m + 1) * 128], y2[kc][:],
                            start=(kc == 0), stop=(kc == C - 1)),
                            waits=([(qE[9 + kc % 2], qE[9 + kc % 2].n)] + ([bF[m]] if kc == 0 else []))
                            if (kc == 0 or m == 0) else [],
                            incs=[bT[m]] if kc == C - 1 else ([eT] if m == 7 else []))
                    umk[kc] = (bT[7], bT[7].n) if kc == C - 1 else (eT, eT.n)
                wmk2[0] = umk[C - 1]
                wmk2[1] = umk[C - 1]
                for m in range(8):
                    plan.op("vector", lambda e, m=m: e.scalar_tensor_tensor(
                        xb2[m][:], bank_ap(m)[:], acol(1, m), xb2[m][:], OP.add, OP.mult),
                        waits=[bT[m], op_ok], incs=[bF[m], eV])
                o2_ok = (eV, eV.n)
                wo2 = {}
                plan.op("sync", lambda e: e.dma_start(out=wt2[0][:], in_=woutTs_d[0:128, :]),
                        waits=[umk[C - 1]], incs=[qE[9]], dma=True)
                for kc in range(C):
                    if kc + 1 < C:
                        plan.op("sync", lambda e, kc=kc: e.dma_start(
                            out=wt2[(kc + 1) % 2][:], in_=woutTs_d[(kc + 1) * 128:(kc + 2) * 128, :]),
                            waits=[wo2.get(kc - 1, umk[C - 1])], incs=[qE[9 + (kc + 1) % 2]], dma=True)
                    for mc in range(8):
                        plan.op("tensor", lambda e, kc=kc, mc=mc: e.matmul(
                            bank_ap(mc)[:], wt2[kc % 2][:, mc * 128:(mc + 1) * 128], xb2[kc][:],
                            start=(kc == 0), stop=(kc == C - 1)),
                            waits=([(qE[9 + kc % 2], qE[9 + kc % 2].n), o2_ok] + ([bF[mc]] if kc == 0 else []))
                            if (kc == 0 or mc == 0) else [],
                            incs=[bT[mc]] if kc == C - 1 else ([eT] if mc == 7 else []))
                    wo2[kc] = (bT[7], bT[7].n) if kc == C - 1 else (eT, eT.n)
                wmk2[0] = wo2[C - 1]
                wmk2[1] = wo2[C - 1]
                for mc in range(8):
                    plan.op("vector", lambda e, mc=mc, t0=t0: e.scalar_tensor_tensor(
                        z[mc][:, t0:t0 + TT], bank_ap(mc)[:], acol(2, mc), z[mc][:, t0:t0 + TT],
                        OP.add, OP.add), waits=[bT[mc]], incs=[bF[mc], eV])
            AR.restore()
            AR.restore()

        ph_ln0()
        barrier()
        if stop_after == "ln0":
            dump_stream_and_finish()
        elif stop_after == "ffn_l0_attn":
            ph_ffn_p1(0)
            barrier()
            dump_stream_and_finish()
        elif stop_after == "ffn_l0":
            ph_ffn_p1(0)
            barrier()
            ph_ffn_p2(0)
            barrier()
            dump_stream_and_finish()
        elif stop_after in ("ffn_l1", "ffn1"):
            for l in range(2):
                ph_ffn_p1(l)
                barrier()
                ph_ffn_p2(l)
                barrier()
            if stop_after == "ffn1":
                ph_addback(x_fm)
                barrier()
            dump_stream_and_finish()
        elif stop_after in ("attn", "full"):
            for l in range(2):
                ph_ffn_p1(l)
                barrier()
                ph_ffn_p2(l)
                barrier()
            ph_addback(x_fm)
            barrier()
            ph_attn()
            barrier()
            if stop_after == "attn":
                dump_stream_and_finish()
            else:
                for c in range(C):
                    for hv in range(2):
                        plan.op("sync", lambda e, c=c, hv=hv: e.dma_start(
                            out=xbk_d[c * 128:(c + 1) * 128, hv * 2048:(hv + 1) * 2048],
                            in_=z[c][:, 1 + hv * 2048:1 + (hv + 1) * 2048]), incs=[qX], dma=True)
                plan.op("sync", None, waits=[(qX, qX.n)])
                barrier()
                ph_ln0()
                barrier()
                for l in range(2):
                    ph_ffn_p1(l)
                    barrier()
                    ph_ffn_p2(l)
                    barrier()
                ph_addback(xbk_d, to_out=True)
        else:
            raise NotImplementedError(stop_after)

        with nc.Block() as block:
            @block.sync
            def _(e):
                plan.emit("sync", e)

            @block.gpsimd
            def _(e):
                plan.emit("gpsimd", e)

            @block.tensor
            def _(e):
                plan.emit("tensor", e)

            @block.vector
            def _(e):
                plan.emit("vector", e)

            @block.scalar
            def _(e):
                plan.emit("scalar", e)

    return nc



def _prep_shared(inp):
    f32 = np.float32
    sh = {}
    sh["ffn_wqkv"] = np.ascontiguousarray(inp["L0_ffn_wqkv"], dtype=f32)
    sh["ffn_wo"] = np.ascontiguousarray(inp["L0_ffn_wo"], dtype=f32)
    w1 = inp["L0_ffn_w1"]; b1 = inp["L0_ffn_b1"]; w2 = inp["L0_ffn_w2"]
    w1p = np.zeros((2, D, 2 * HIDP), f32)
    w1p[:, :, 0:HID] = w1[:, :, 0:HID]
    w1p[:, :, HIDP:HIDP + HID] = w1[:, :, HID:2 * HID]
    sh["ffn_w1p"] = w1p
    b1p = np.zeros((2, 2 * HIDP), f32)
    b1p[:, 0:HID] = b1[:, 0:HID]
    b1p[:, HIDP:HIDP + HID] = b1[:, HID:2 * HID]
    w2p = np.zeros((2, HIDP, D), f32)
    w2p[:, 0:HID, :] = w2
    sh["ffn_w2p"] = w2p
    fc = np.zeros((2, 128, M1 + 3 * C), f32)
    for l in range(2):
        fc[l, :, 0:M1] = b1p[l].reshape(M1, 128).T
        fc[l, :, M1:M1 + C] = inp["L0_ffn_b2"][l].reshape(C, 128).T
        fc[l, :, M1 + C:M1 + 2 * C] = inp["L0_ffn_ln_g"][l].reshape(C, 128).T
        fc[l, :, M1 + 2 * C:M1 + 3 * C] = inp["L0_ffn_ln_b"][l].reshape(C, 128).T
    sh["ffn_cols"] = fc
    qs = np.zeros((2, 128, 4), f32)
    for l in range(2):
        q_ = np.tile(inp["L0_ffn_q_scale"][l], 2)
        k_ = np.tile(inp["L0_ffn_k_scale"][l], 2)
        qsw = q_.reshape(4, 32)[[1, 0, 3, 2]].reshape(128)
        ksw = k_.reshape(4, 32)[[1, 0, 3, 2]].reshape(128)
        qs[l] = np.stack([q_, qsw, k_, ksw], axis=1)
    sh["ffn_qscol"] = qs
    sh["attn_wqkvuT"] = np.ascontiguousarray(inp["L0_attn_Wqkvu"].T, dtype=f32)
    sh["attn_woutT"] = np.ascontiguousarray(inp["L0_attn_Wout"].T, dtype=f32)
    b = inp["L0_attn_bqkvu"]
    ac = np.zeros((128, 4 * C), f32)
    ac[:, 0:C] = b[0:D].reshape(C, 128).T
    ac[:, C:2 * C] = b[3 * D:4 * D].reshape(C, 128).T
    ac[:, 2 * C:3 * C] = inp["L0_attn_bout"].reshape(C, 128).T
    ac[:, 3 * C:4 * C] = inp["L0_attn_ln_w"].reshape(C, 128).T
    sh["attn_cols"] = ac
    sh["attn_lnb"] = np.ascontiguousarray(inp["L0_attn_ln_b"].reshape(C, 128).T, dtype=f32)
    sh["attn_bkv_row"] = np.ascontiguousarray(b[D:3 * D].reshape(1, 2 * D), dtype=f32)
    sh["attn_theta_row"] = np.ascontiguousarray(np.asarray(inp["L0_attn_theta"], f32).reshape(1, D))
    sh["idx_row"] = np.arange(N, dtype=f32).reshape(1, N)

    t = np.arange(N, dtype=np.float64)
    half = np.arange(0, 64, 2, dtype=np.float64)
    invf = 1.0 / 10000.0 ** (half / 64.0)
    freqs = t[:, None] * invf[None, :]
    base = (half + 0.4 * 64) / (1.4 * 64)
    power = (t - N // 2) / 256.0
    scale = base[None, :] ** power[:, None]
    cos = np.cos(freqs); sin = np.sin(freqs)
    rows32 = lambda a: a.T
    csq = rows32(cos * scale) * 8.0
    snq = rows32(sin * scale) * 8.0
    csk = rows32(cos / scale)
    snk = rows32(sin / scale)
    sgn = np.concatenate([-np.ones((32, 1)), np.ones((32, 1))], axis=1)
    rot = np.zeros((4, 128, N), f32)
    for g in range(4):
        s = -1.0 if (g % 2 == 0) else 1.0
        rot[0, g * 32:(g + 1) * 32] = csq
        rot[1, g * 32:(g + 1) * 32] = s * snq
        rot[2, g * 32:(g + 1) * 32] = csk
        rot[3, g * 32:(g + 1) * 32] = s * snk
    sh["rot_tabs"] = rot

    p64 = np.zeros((128, 128), f32)
    for m in range(128):
        blk, off = divmod(m, 64)
        p64[blk * 64 + (off + 32) % 64, m] = 1.0
    sh["p64_swap"] = p64

    a = np.arange(128)
    m0 = np.where(a[:, None] >= a[None, :], 0.0, -1e30).astype(f32)
    m4 = np.where(a[:, None] <= a[None, :], 0.0, -1e30).astype(f32)
    sh["tri_masks"] = np.stack([m0, m4])

    selh = np.zeros((8, 8 * 128), f32)
    for h in range(8):
        selh[h, h * 128:(h + 1) * 128] = 1.0
    sh["sel_h"] = selh
    selblk = np.zeros((8, 128, 104), f32)
    for c in range(8):
        for p in range(128):
            hrow = 2 * (c % 4) + p // 64 + (64 if c >= 4 else 0)
            selblk[c, p, hrow] = 1.0
            selblk[c, p, hrow + 32] = 1.0
    sh["sel_blk"] = selblk
    selrep = np.zeros((4, 8, 128), f32)
    for c in range(4):
        selrep[c, 2 * c, 0:64] = 1.0
        selrep[c, 2 * c + 1, 64:128] = 1.0
    sh["sel_rep"] = selrep
    ones = np.zeros((6, 128), f32)
    ones[0] = 1.0
    ones[1] = EPS
    ones[2] = 1.0
    ones[3] = np.pi / 2
    ones[4] = 1e-30
    sh["ones_misc"] = ones
    sh["ones_row"] = np.ones((1, 128), f32)
    return sh


def kernel(**inp):
    stop_after = inp.pop("_stop_after", "full")
    x = np.asarray(inp["L0_attn_x"], dtype=np.float32)
    sh = _prep_shared(inp)
    nc = build(stop_after)
    in_maps = []
    for b in range(8):
        m = dict(sh)
        m["x_fm"] = np.ascontiguousarray(x[b].T)
        in_maps.append(m)
    res = run_bass_kernel_spmd(nc, in_maps, core_ids=list(range(8)))
    out = np.stack([np.asarray(r["out_fm"]).T for r in res.results], axis=0)
    return out.astype(np.float32)


# version 16 (final)
# speedup vs baseline: 1.1702x; 1.1702x over previous
import numpy as np
import concourse.bass as bass
import concourse.mybir as mybir
from concourse.alu_op_type import AluOpType as OP
from concourse.bass_utils import run_bass_kernel_spmd

F32 = mybir.dt.float32
BF16 = mybir.dt.bfloat16
I32 = mybir.dt.int32
AF = mybir.ActivationFunctionType

N = 4096
D = 1024
C = 8
TT = 512
NT = N // TT
W = 512
NW = N // W
SCOLS = N + 4
HID = 2730
HIDP = 2816
M1 = 44
M1H = 22
INNER = 512
EPS = 1e-5
INV2PI = float(1.0 / (2.0 * np.pi))
TWOPI = float(2.0 * np.pi)
PI_ = float(np.pi)

ENGS = ("sync", "gpsimd", "tensor", "vector", "scalar")


class Chan:
    def __init__(self, sem, step=1):
        self.sem = sem
        self.step = step
        self.n = 0

    def inc(self):
        self.n += self.step
        return (self.sem, self.step)


class Plan:
    def __init__(self):
        self.seq = {e: [] for e in ENGS}
        self.selfch = {}
        self.keepalive = None

    def op(self, eng, fn, waits=(), incs=(), dma=False):
        rw = []
        for w_ in waits:
            if isinstance(w_, tuple):
                rw.append((w_[0].sem, w_[1]))
            else:
                rw.append((w_.sem, w_.n))
        sc = self.selfch.get(eng)
        if eng == "vector" and fn is not None and not dma and self.keepalive is not None:
            self.vcount = getattr(self, "vcount", 0) + 1
            if self.vcount % 4 == 0:
                ka, kach = self.keepalive
                self.seq["tensor"].append((ka, [(kach.sem, kach.n - 24)], [], False))
        if sc is not None and fn is not None and not dma:
            rw.append((sc.sem, sc.n))
            incs = list(incs)
            if sc not in incs:
                incs.append(sc)
        ri = [c.inc() for c in incs]
        self.seq[eng].append((fn, rw, ri, dma))

    def emit(self, eng_name, e):
        selfsem = self.selfch[eng_name].sem.num if eng_name in self.selfch else None
        hw = {}
        for fn, waits, incs, dma in self.seq[eng_name]:
            for (sem, v) in waits:
                if v <= 0:
                    continue
                key = sem.num
                if key != selfsem and hw.get(key, -1) >= v:
                    continue
                e.wait_ge(sem, v)
                if key != selfsem:
                    hw[key] = v
            if fn is not None:
                inst = fn(e)
                if incs:
                    inst.then_inc(*incs[0])
                    for (sem, s) in incs[1:]:
                        e.maybe_drain_then_inc((sem, s))
            else:
                for (sem, s) in incs:
                    e.maybe_drain_then_inc((sem, s))


def _dt_bytes(dt):
    return {F32: 4, BF16: 2, I32: 4}[dt]


class Arena:

    def __init__(self, t, nbytes):
        self.t = t
        self.nbytes = nbytes
        self.top = 0
        self.marks = []
        self.peak = 0

    def save(self):
        self.marks.append(self.top)

    def restore(self):
        self.top = self.marks.pop()

    def alloc(self, parts, cols, dt):
        start = (self.top + 31) & ~31
        b = cols * _dt_bytes(dt)
        self.top = start + b
        self.peak = max(self.peak, self.top)
        assert self.top <= self.nbytes, f"arena overflow {self.top} > {self.nbytes}"
        v = self.t[0:parts, start // 2:(start + b) // 2]
        if dt != BF16:
            v = v.bitcast(dt)
        return v


def hcols(w, c, n=W):
    base = w * W + 1 if c < 4 else w * W
    return (base, base + n)


def build(stop_after="full"):
    nc = bass.Bass("TRN2", target_bir_lowering=False)

    x_fm = nc.dram_tensor("x_fm", [D, N], F32, kind="ExternalInput").ap()
    wqkv_d = nc.dram_tensor("ffn_wqkv", [2, D, 3 * INNER], F32, kind="ExternalInput").ap()
    wo_d = nc.dram_tensor("ffn_wo", [2, INNER, D], F32, kind="ExternalInput").ap()
    w1_d = nc.dram_tensor("ffn_w1p", [2, D, 2 * HIDP], F32, kind="ExternalInput").ap()
    w2_d = nc.dram_tensor("ffn_w2p", [2, HIDP, D], F32, kind="ExternalInput").ap()
    fcols_d = nc.dram_tensor("ffn_cols", [2, 128, M1 + C + C + C], F32, kind="ExternalInput").ap()
    qscol_d = nc.dram_tensor("ffn_qscol", [2, 128, 4], F32, kind="ExternalInput").ap()
    wquT_d = nc.dram_tensor("attn_wqkvuT", [D, 4 * D], F32, kind="ExternalInput").ap()
    woutT_d = nc.dram_tensor("attn_woutT", [D, D], F32, kind="ExternalInput").ap()
    acols_d = nc.dram_tensor("attn_cols", [128, 4 * C], F32, kind="ExternalInput").ap()
    alnb_d = nc.dram_tensor("attn_lnb", [128, C], F32, kind="ExternalInput").ap()
    bkv_d = nc.dram_tensor("attn_bkv_row", [1, 2 * D], F32, kind="ExternalInput").ap()
    theta_d = nc.dram_tensor("attn_theta_row", [1, D], F32, kind="ExternalInput").ap()
    idx_d = nc.dram_tensor("idx_row", [1, N], F32, kind="ExternalInput").ap()
    rot_d = nc.dram_tensor("rot_tabs", [4, 128, N], F32, kind="ExternalInput").ap()
    p64_d = nc.dram_tensor("p64_swap", [128, 128], F32, kind="ExternalInput").ap()
    masks_d = nc.dram_tensor("tri_masks", [2, 128, 128], F32, kind="ExternalInput").ap()
    selh_d = nc.dram_tensor("sel_h", [8, 8 * 128], F32, kind="ExternalInput").ap()
    selblk_d = nc.dram_tensor("sel_blk", [8, 128, 104], F32, kind="ExternalInput").ap()
    selrep_d = nc.dram_tensor("sel_rep", [4, 8, 128], F32, kind="ExternalInput").ap()
    ones_d = nc.dram_tensor("ones_misc", [6, 128], F32, kind="ExternalInput").ap()
    onesrow_d = nc.dram_tensor("ones_row", [1, 128], F32, kind="ExternalInput").ap()

    xbk_d = nc.dram_tensor("xbk", [D, N], F32, kind="Internal").ap()
    w1s_d = nc.dram_tensor("w1s", [2, D, 2 * HIDP], BF16, kind="Internal").ap()
    w2s_d = nc.dram_tensor("w2s", [2, HIDP, D], BF16, kind="Internal").ap()
    wqkvs_d = nc.dram_tensor("wqkvs", [2, D, 3 * INNER], BF16, kind="Internal").ap()
    wos_d = nc.dram_tensor("wos", [2, INNER, D], BF16, kind="Internal").ap()
    wquTs_d = nc.dram_tensor("wquTs", [D, 4 * D], BF16, kind="Internal").ap()
    woutTs_d = nc.dram_tensor("woutTs", [D, D], BF16, kind="Internal").ap()
    out_d = nc.dram_tensor("out_fm", [D, N], F32, kind="ExternalOutput").ap()

    plan = Plan()
    from contextlib import ExitStack
    with ExitStack() as ST:
        def sb(name, shape, dt=F32):
            return ST.enter_context(nc.sbuf_tensor(name, list(shape), dt))

        def psum(name, shape, dt=F32):
            return ST.enter_context(nc.psum_tensor(name, list(shape), dt))

        def chan(name, step=1):
            return Chan(ST.enter_context(nc.semaphore(name)), step)

        z = [sb(f"z{c}", (128, SCOLS)) for c in range(C)]
        p64 = sb("p64", (128, 128), BF16)
        masks = sb("masks", (128, 2 * 128))
        selh = sb("selh", (8, 8 * 128), BF16)
        selblk = sb("selblk", (128, 8 * 104), BF16)
        selrep = sb("selrep", (8, 4 * 128), BF16)
        onescol_bf = sb("onescol_bf", (128, 1), BF16)
        onescol_f = sb("onescol_f", (128, 1))
        epscol = sb("epscol", (128, 1))
        onesrow_bf = sb("onesrow_bf", (1, 128), BF16)
        halfpicol = sb("halfpicol", (128, 1))
        tiny30col = sb("tiny30col", (128, 1))
        onesrow_f = sb("onesrow_f", (1, 128))
        fcols = sb("fcols", (128, 2 * (M1 + 3 * C)))
        qscols = sb("qscols", (128, 8))
        acols = sb("acols", (128, 4 * C))
        alnb = sb("alnb", (128, C))
        wselblk = sb("wselblk", (128, 8 * 104), BF16)
        arena_t = sb("arena", (128, 62464 // 2), BF16)
        AR = Arena(arena_t, 62464)

        pA = psum("pA", (128, 1024))
        pB = psum("pB", (128, 1024))
        pC_ = [psum(f"pC{i}", (128, 512)) for i in range(4)]

        _bt = [chan(f"bT{i}") for i in range(6)]
        _bf = [chan(f"bF{i}") for i in range(6)]
        bT = [_bt[0], _bt[0], _bt[1], _bt[1], _bt[2], _bt[3], _bt[4], _bt[5]]
        bF = [_bf[0], _bf[0], _bf[1], _bf[1], _bf[2], _bf[3], _bf[4], _bf[5]]
        eT = chan("eT"); eV = chan("eV"); eS = chan("eS"); eG = chan("eG")
        plan.selfch = {"vector": eV, "scalar": eS, "gpsimd": eG}
        plan.keepalive = (lambda e: e.ldweights(onescol_bf[:]), eV)
        qX = chan("qX", 16)
        qW = chan("qW", 16)
        qD = [chan(f"qD{i}", 16) for i in range(8)]
        qE = [chan(f"qE{i}", 16) for i in range(11)]

        BANK = {}
        def bank_ap(i):
            if i < 2:
                return pA[:, i * 512:(i + 1) * 512]
            if i < 4:
                return pB[:, (i - 2) * 512:(i - 1) * 512]
            return pC_[i - 4][:]

        def barrier():
            snap = [(eT, eT.n), (eV, eV.n), (eS, eS.n), (eG, eG.n), (qX, qX.n), (qW, qW.n)] + [
                (q, q.n) for q in qD] + [(q, q.n) for q in qE]
            for eng, ch in (("tensor", eT), ("vector", eV), ("scalar", eS), ("gpsimd", eG)):
                plan.op(eng, None, waits=[s for s in snap if s[0] is not ch], incs=[ch])
            plan.op("sync", None, waits=[(eT, eT.n), (eV, eV.n), (eS, eS.n), (eG, eG.n)])

        def dma(eng, q, out, in_):
            plan.op(eng, lambda e, o=out, i=in_: e.dma_start(out=o, in_=i), incs=[q], dma=True)

        dma("gpsimd", qW, p64[:], p64_d)
        dma("sync", qX, masks[:].rearrange("p (m k) -> p m k", m=2, k=128), masks_d.rearrange("m p k -> p m k"))
        dma("gpsimd", qW, selh[:], selh_d)
        dma("gpsimd", qW, selblk[:].rearrange("p (c k) -> p c k", c=8, k=104), selblk_d.rearrange("c p k -> p c k"))
        dma("gpsimd", qW, selrep[:].rearrange("p (c k) -> p c k", c=4, k=128), selrep_d.rearrange("c p k -> p c k"))
        dma("gpsimd", qW, onescol_bf[:], ones_d[0:1, :].rearrange("o p -> p o"))
        dma("gpsimd", qW, onesrow_bf[:], onesrow_d)
        dma("sync", qX, onescol_f[:], ones_d[2:3, :].rearrange("o p -> p o"))
        dma("sync", qX, epscol[:], ones_d[1:2, :].rearrange("o p -> p o"))
        dma("sync", qX, halfpicol[:], ones_d[3:4, :].rearrange("o p -> p o"))
        dma("sync", qX, tiny30col[:], ones_d[4:5, :].rearrange("o p -> p o"))
        dma("sync", qX, onesrow_f[:], onesrow_d)
        dma("sync", qX, fcols[:].rearrange("p (l m) -> p l m", l=2, m=M1 + 3 * C), fcols_d.rearrange("l p m -> p l m"))
        dma("sync", qX, qscols[:].rearrange("p (l m) -> p l m", l=2, m=4), qscol_d.rearrange("l p m -> p l m"))
        dma("sync", qX, acols[:], acols_d)
        dma("sync", qX, alnb[:], alnb_d)
        n_const_w = qW.n
        n_const_x = qX.n

        def fcol(l, j):
            return fcols[:, l * (M1 + 3 * C) + j: l * (M1 + 3 * C) + j + 1]

        def b1col(l, m): return fcol(l, m)
        def b2col(l, c): return fcol(l, M1 + c)
        def lngcol(l, c): return fcol(l, M1 + C + c)
        def lnbcol(l, c): return fcol(l, M1 + 2 * C + c)
        def qscol(l, j): return qscols[:, l * 4 + j: l * 4 + j + 1]
        def acol(j, c): return acols[:, j * C + c: j * C + c + 1]

        for c in range(C):
            dma("sync", qX, z[c][:, 1:N + 1], x_fm[c * 128:(c + 1) * 128, :])
            plan.op("vector", lambda e, c=c: e.memset(z[c][:, 0:1], 0.0))
            plan.op("vector", lambda e, c=c: e.memset(z[c][:, N + 1:SCOLS], 0.0), incs=[eV])
        x_loaded_x = qX.n

        plan.op("tensor", None, waits=[(qX, qX.n), (qW, qW.n), (eV, eV.n)], incs=[eT])
        plan.op("vector", None, waits=[(qX, qX.n), (qW, qW.n)], incs=[eV])
        plan.op("scalar", None, waits=[(qX, qX.n), (qW, qW.n), (eV, eV.n)], incs=[eS])
        plan.op("gpsimd", None, waits=[(qX, qX.n), (eV, eV.n)], incs=[eG])

        AR.save()
        AR.alloc(128, (AR.nbytes - 2 * (2 * HIDP * 2) - 64) // 2, BF16)
        cstg = [AR.alloc(128, 2 * HIDP, BF16) for _ in range(2)]
        cmk = {}
        cjobs = []
        for r in range(C):
            cjobs.append((wqkv_d[0, r * 128:(r + 1) * 128, :], wqkvs_d[0, r * 128:(r + 1) * 128, :], 3 * INNER))
        for r in range(4):
            cjobs.append((wo_d[0, r * 128:(r + 1) * 128, :], wos_d[0, r * 128:(r + 1) * 128, :], D))
        for l in range(2):
            for r in range(C):
                cjobs.append((w1_d[l, r * 128:(r + 1) * 128, :], w1s_d[l, r * 128:(r + 1) * 128, :], 2 * HIDP))
        for l in range(2):
            for r in range(HIDP // 128):
                cjobs.append((w2_d[l, r * 128:(r + 1) * 128, :], w2s_d[l, r * 128:(r + 1) * 128, :], D))
        for r in range(C):
            cjobs.append((wqkv_d[1, r * 128:(r + 1) * 128, :], wqkvs_d[1, r * 128:(r + 1) * 128, :], 3 * INNER))
        for r in range(4):
            cjobs.append((wo_d[1, r * 128:(r + 1) * 128, :], wos_d[1, r * 128:(r + 1) * 128, :], D))
        for r in range(C):
            cjobs.append((wquT_d[r * 128:(r + 1) * 128, :], wquTs_d[r * 128:(r + 1) * 128, :], 4 * D))
        for r in range(C):
            cjobs.append((woutT_d[r * 128:(r + 1) * 128, :], woutTs_d[r * 128:(r + 1) * 128, :], D))
        cmark_at = {11: "ffn0", 11 + 16: "w1", 27 + 44: "mlp", 71 + 12: "ffn1", len(cjobs) - 1: "all"}
        conv_marks = {}
        for i, (srcap, dstap, cols) in enumerate(cjobs):
            sl = i % 2
            plan.op("gpsimd", lambda e, srcap=srcap, sl=sl, cols=cols: e.dma_start(
                out=cstg[sl][:, 0:cols], in_=srcap), waits=[cmk.get(sl, (eT, 0))], incs=[qD[5 + sl]], dma=True)
            plan.op("sync", lambda e, dstap=dstap, sl=sl, cols=cols: e.dma_start(
                out=dstap, in_=cstg[sl][:, 0:cols]), waits=[(qD[5 + sl], qD[5 + sl].n)],
                incs=[qE[5 + sl]], dma=True)
            cmk[sl] = (qE[5 + sl], qE[5 + sl].n)
            if i in cmark_at:
                conv_marks[cmark_at[i]] = [(qE[5], qE[5].n), (qE[6], qE[6].n)]
        AR.restore()
        conv_done = conv_marks["all"]


        def ln_stats_tile(R_, src_bf, sq_bf, n=TT, sums_bank=4, sq_bank=5):
            bs, bq = bank_ap(sums_bank), bank_ap(sq_bank)
            for c in range(C):
                plan.op("tensor", lambda e, c=c, bs=bs, n=n: e.matmul(
                    bs[0:1, 0:n], onescol_bf[:], src_bf[c][:, 0:n], start=(c == 0), stop=(c == C - 1)),
                    waits=[bF[sums_bank]] if c == 0 else (),
                    incs=[bT[sums_bank]] if c == C - 1 else ())
                plan.op("tensor", lambda e, c=c, bq=bq, n=n: e.matmul(
                    bq[0:1, 0:n], onescol_bf[:], sq_bf[c][:, 0:n], start=(c == 0), stop=(c == C - 1)),
                    waits=[bF[sq_bank]] if c == 0 else (),
                    incs=[bT[sq_bank]] if c == C - 1 else ())
            mrow = lambda k: R_[0:1, k * n: (k + 1) * n]
            plan.op("vector", lambda e: e.tensor_scalar(mrow(2), bs[0:1, 0:n], 1.0 / D, None, OP.mult),
                    waits=[bT[sums_bank]])
            plan.op("vector", lambda e: e.tensor_scalar(mrow(3), bq[0:1, 0:n], 1.0 / D, None, OP.mult),
                    waits=[bT[sq_bank]], incs=[eV])
            plan.op("scalar", lambda e: e.activation(mrow(4), mrow(2), AF.Square), waits=[eV], incs=[eS])
            plan.op("vector", lambda e: e.tensor_tensor(mrow(3), mrow(3), mrow(4), OP.subtract), waits=[eS], incs=[eV])
            plan.op("scalar", lambda e: e.activation(mrow(3), mrow(3), AF.Ln, bias=epscol[0:1, :]),
                    waits=[eV], incs=[bF[sums_bank]])
            plan.op("scalar", lambda e: e.activation(mrow(0), mrow(3), AF.Exp, scale=-0.5),
                    incs=[eS, bF[sq_bank]])
            plan.op("vector", lambda e: e.tensor_tensor(mrow(1), mrow(2), mrow(0), OP.mult), waits=[eS], incs=[eV])
            return mrow(0), mrow(1)

        def replicate_rows(r_row, mr_row, n, rep_bank, hilo):
            b0, b1 = bank_ap(rep_bank), bank_ap(rep_bank + 1)
            _hi, _lo = hilo[0:1, 0:n], hilo[0:1, n:2 * n]
            plan.op("vector", lambda e, _hi=_hi: e.tensor_copy(_hi, r_row))
            plan.op("vector", lambda e, _lo=_lo, _hi=_hi: e.tensor_tensor(_lo, r_row, _hi, OP.subtract),
                    incs=[eV])
            plan.op("tensor", lambda e, _hi=_hi, b0=b0: e.matmul(b0[:, 0:n], onesrow_bf[:], _hi,
                    start=True, stop=False), waits=[bF[rep_bank], (eV, eV.n)])
            plan.op("tensor", lambda e, _lo=_lo, b0=b0: e.matmul(b0[:, 0:n], onesrow_bf[:], _lo,
                    start=False, stop=True), incs=[bT[rep_bank]])
            mk0 = (bT[rep_bank], bT[rep_bank].n)
            plan.op("vector", lambda e, _hi=_hi: e.tensor_copy(_hi, mr_row), waits=[mk0])
            plan.op("vector", lambda e, _lo=_lo, _hi=_hi: e.tensor_tensor(_lo, mr_row, _hi, OP.subtract),
                    incs=[eV])
            plan.op("tensor", lambda e, _hi=_hi, b1=b1: e.matmul(b1[:, 0:n], onesrow_bf[:], _hi,
                    start=True, stop=False), waits=[(eV, eV.n)])
            plan.op("tensor", lambda e, _lo=_lo, b1=b1: e.matmul(b1[:, 0:n], onesrow_bf[:], _lo,
                    start=False, stop=True), incs=[bT[rep_bank]])
            return b0, b1

        def ph_ln0():
            AR.save()
            xb = [AR.alloc(128, TT, BF16) for _ in range(C)]
            xq = [AR.alloc(128, TT, BF16) for _ in range(C)]
            rows = AR.alloc(1, 5 * TT, F32)
            for t in range(NT):
                t0 = t * TT + 1
                for c in range(C):
                    plan.op("vector", lambda e, c=c, t0=t0: e.tensor_copy(xb[c][:], z[c][:, t0:t0 + TT]),
                            incs=[eV])
                    plan.op("scalar", lambda e, c=c: e.activation(xq[c][:], xb[c][:], AF.Square),
                            waits=[(eV, eV.n)], incs=[eS] if c == C - 1 else ())
                plan.op("tensor", None, waits=[(eV, eV.n), (eS, eS.n)])
                r_row, mr_row = ln_stats_tile(rows, xb, xq)
                b0, b1 = replicate_rows(r_row, mr_row, TT, 0, rows[0:1, 4 * TT:5 * TT].bitcast(BF16))
                for c in range(C):
                    plan.op("vector", lambda e, c=c, t0=t0, b0=b0: e.tensor_tensor(
                        z[c][:, t0:t0 + TT], z[c][:, t0:t0 + TT], b0[:, 0:TT], OP.mult), waits=[bT[0]])
                    plan.op("vector", lambda e, c=c, t0=t0, b1=b1: e.tensor_tensor(
                        z[c][:, t0:t0 + TT], z[c][:, t0:t0 + TT], b1[:, 0:TT], OP.subtract),
                        incs=[bF[0], eV] if c == C - 1 else ())
            AR.restore()

        def dump_stream_and_finish(src=None):
            for c in range(C):
                plan.op("sync", lambda e, c=c: e.dma_start(out=out_d[c * 128:(c + 1) * 128, :], in_=z[c][:, 1:N + 1]),
                        waits=[(eV, eV.n), (eT, eT.n), (eS, eS.n), (eG, eG.n)], incs=[qX], dma=True)
            plan.op("sync", None, waits=[(qX, qX.n)])

        def ph_ffn_p1(l):
            AR.save()
            h_bf = [AR.alloc(128, W, BF16) for _ in range(C)]
            tabs = h_bf[0:4]
            o_sb = h_bf[4:8]
            qraw0 = AR.alloc(128, W, BF16)
            sq_bf = [AR.alloc(128, W, BF16) for _ in range(2)]
            rows_q = AR.alloc(8, W, F32)
            rows_k = rows_q
            wrow = AR.alloc(8, W, F32)
            nbq_f = AR.alloc(8, W, BF16)
            nbk_f = AR.alloc(8, W, BF16)
            rnq_bf = AR.alloc(8, W, BF16)
            rnk_bf = AR.alloc(8, W, BF16)
            b8_bf = AR.alloc(8, W, BF16)
            mkc = AR.alloc(8, 8, F32)
            a_st = AR.alloc(128, W, F32)
            t1_st = a_st
            t2_st = AR.alloc(128, 640, F32)
            es2_st = AR.alloc(128, 640, F32)
            qsb = [AR.alloc(128, W, BF16) for _ in range(4)]
            ksb = [[AR.alloc(128, W, BF16) for _ in range(4)] for _ in range(2)]
            vsb = [[AR.alloc(128, 520, BF16) for _ in range(4)] for _ in range(2)]
            e_bf = [AR.alloc(128, 640, BF16) for _ in range(4)]
            brep_bf = AR.alloc(128, W, BF16)
            o_un = AR.alloc(64, W, BF16)
            rcp_f = rows_q[0:1, :]
            wkv = [AR.alloc(128, 1024, BF16) for _ in range(2)]
            wob = [AR.alloc(128, 4 * 128, BF16) for _ in range(2)]

            plan.op("vector", lambda e: e.tensor_copy(wselblk[:], selblk[:]), waits=[(eG, eG.n)])
            for c in range(C):
                scol = qscol(l, 0) if c < 4 else qscol(l, 2)
                plan.op("scalar", lambda e, scol=scol: e.activation(a_st[:, 0:1], scol, AF.Square),
                        waits=[(eV, eV.n)], incs=[eS])
                grp = 32 if c < 4 else 96
                plan.op("vector", lambda e, c=c, grp=grp: e.tensor_scalar(
                    wselblk[:, c * 104 + grp:c * 104 + grp + 8], selblk[:, c * 104 + grp:c * 104 + grp + 8],
                    a_st[:, 0:1], None, OP.mult), waits=[(eS, eS.n)], incs=[eV])
            for i in range(2):
                for ts in range(4):
                    plan.op("vector", lambda e, i=i, ts=ts: e.memset(
                        vsb[i][ts][:].rearrange("p (h e) -> p h e", h=8, e=65)[:, :, 64:65], 1.0), incs=[eV])
            stash = AR.alloc(128, 32, F32)
            for c in range(4, 8):
                plan.op("vector", lambda e, c=c: e.tensor_copy(
                    stash[:, (c - 4) * 8:(c - 4) * 8 + 8],
                    z[c][:, 0:4096].rearrange("p (t k) -> p t k", k=512)[:, :, 0]), incs=[eV])

            Wq = wqkvs_d[l]
            prev_w_mark = (eT, 0)
            prev_wo_mark = (eT, 0)
            for w in range(NW):
                pw = w % 2
                for c in range(C):
                    if c < 4:
                        c0, c1 = hcols(w, c)
                        plan.op("vector", lambda e, c=c, c0=c0, c1=c1: e.tensor_copy(h_bf[c][:], z[c][:, c0:c1]),
                                incs=[eV])
                    else:
                        plan.op("vector", lambda e, c=c, w=w: e.tensor_copy(
                            h_bf[c][:, 1:W], z[c][:, w * W + 1:w * W + W]))
                        plan.op("vector", lambda e, c=c, w=w: e.tensor_copy(
                            h_bf[c][:, 0:1], stash[:, (c - 4) * 8 + w:(c - 4) * 8 + w + 1]), incs=[eV])
                hdone = (eV, eV.n)
                vm = {}
                plan.op("sync", lambda e: e.dma_start(out=wkv[0][:, 0:512], in_=Wq[0:128, 1024:1536]),
                        waits=[prev_w_mark] + (conv_marks['ffn0'] if l == 0 and w == 0 else (conv_marks['ffn1'] if l == 1 and w == 0 else [])), incs=[qE[9]], dma=True)
                for kc in range(C):
                    if kc + 1 < C:
                        plan.op("sync", lambda e, kc=kc: e.dma_start(
                            out=wkv[(kc + 1) % 2][:, 0:512], in_=Wq[(kc + 1) * 128:(kc + 2) * 128, 1024:1536]),
                            waits=[vm.get(kc - 1, prev_w_mark)], incs=[qE[9 + (kc + 1) % 2]], dma=True)
                    wready = (qE[9 + kc % 2], qE[9 + kc % 2].n)
                    for ts in range(4):
                        plan.op("tensor", lambda e, kc=kc, ts=ts: e.matmul(
                            pC_[ts][:], h_bf[kc][:, ts * 128:(ts + 1) * 128], wkv[kc % 2][:, 0:512],
                            start=(kc == 0), stop=(kc == C - 1)),
                            waits=([wready, hdone, bF[4 + ts]] if kc == 0 else ([wready] if ts == 0 else [])),
                            incs=[bT[4 + ts]] if kc == C - 1 else ([eT] if ts == 3 else []))
                    vm[kc] = (bT[7], bT[7].n) if kc == C - 1 else (eT, eT.n)
                for ts in range(4):
                    plan.op("vector", lambda e, ts=ts, pw=pw: e.tensor_copy(
                        vsb[pw][ts][:].rearrange("p (h e) -> p h e", h=8, e=65)[:, :, 0:64],
                        pC_[ts][:].rearrange("p (h e) -> p h e", h=8, e=64)),
                        waits=[bT[4 + ts]], incs=[bF[4 + ts], eV])
                qm = {}
                plan.op("sync", lambda e: e.dma_start(out=wkv[0][:], in_=Wq[0:128, 0:1024]),
                        waits=[vm[C - 2]], incs=[qE[9]], dma=True)
                for kc in range(C):
                    if kc + 1 < C:
                        plan.op("sync", lambda e, kc=kc: e.dma_start(
                            out=wkv[(kc + 1) % 2][:], in_=Wq[(kc + 1) * 128:(kc + 2) * 128, 0:1024]),
                            waits=[qm.get(kc - 1, vm[C - 1])], incs=[qE[9 + (kc + 1) % 2]], dma=True)
                    wready = (qE[9 + kc % 2], qE[9 + kc % 2].n)
                    for m in range(8):
                        plan.op("tensor", lambda e, kc=kc, m=m: e.matmul(
                            bank_ap(m)[:], wkv[kc % 2][:, m * 128:(m + 1) * 128], h_bf[kc][:],
                            start=(kc == 0), stop=(kc == C - 1)),
                            waits=([wready, bF[m]] if kc == 0 else ([wready] if m == 0 else [])),
                            incs=[bT[m]] if kc == C - 1 else ([eT] if m == 7 else []))
                    qm[kc] = (bT[7], bT[7].n) if kc == C - 1 else (eT, eT.n)
                prev_w_mark = qm[C - 1]
                for j in range(4):
                    plan.op("gpsimd", lambda e, j=j, w=w: e.dma_start(
                        out=tabs[j][:], in_=rot_d[j, :, w * W:(w + 1) * W]),
                        waits=[qm[C - 1]] if j == 0 else [], incs=[qD[2]], dma=True)
                tabs_ok = (qD[2], qD[2].n)
                plan.op("vector", lambda e: e.tensor_copy(qraw0[:], bank_ap(0)[:]), waits=[bT[0]],
                        incs=[bF[0], eV])
                bsums = bank_ap(0)
                for m in range(8):
                    src = qraw0[:] if m == 0 else bank_ap(m)[:]
                    plan.op("scalar", lambda e, m=m, src=src: e.activation(sq_bf[m % 2][:], src, AF.Square),
                            waits=([(eV, eV.n)] if m == 0 else [bT[m]]) + [(eT, eT.n)], incs=[eS])
                    plan.op("tensor", lambda e, m=m: e.matmul(
                        bsums[0:104, :], wselblk[:, m * 104:m * 104 + 104], sq_bf[m % 2][:],
                        start=(m == 0), stop=(m == 7)),
                        waits=[(eS, eS.n)] + ([bF[0]] if m == 0 else []),
                        incs=[bT[0]] if m == 7 else [eT])
                def norm_chain(grp0, rows, rn_bf_t, nb_t):
                    plan.op("scalar", lambda e, grp0=grp0, rows=rows: e.activation(
                        rows[:], bsums[grp0:grp0 + 8, :], AF.Ln), waits=[bT[0], (eV, eV.n)], incs=[eS])
                    plan.op("scalar", lambda e, grp0=grp0: e.activation(
                        wrow[:], bsums[grp0 + 32:grp0 + 40, :], AF.Ln), incs=[eS])
                    plan.op("vector", lambda e, rows=rows: e.tensor_tensor(
                        wrow[:], wrow[:], rows[:], OP.subtract), waits=[(eS, eS.n)], incs=[eV])
                    plan.op("scalar", lambda e, rows=rows: e.activation(rows[:], rows[:], AF.Exp, scale=-0.5),
                            waits=[(eV, eV.n)], incs=[eS])
                    plan.op("scalar", lambda e, nb_t=nb_t: e.activation(nb_t[:], wrow[:], AF.Exp, scale=0.5),
                            incs=[eS])
                    plan.op("vector", lambda e, rows=rows, rn_bf_t=rn_bf_t: e.tensor_scalar(
                        rn_bf_t[:], rows[:], 1e12, None, OP.min), waits=[(eS, eS.n)], incs=[eV])
                norm_chain(0, rows_q, rnq_bf, nbq_f)
                norm_chain(64, rows_k, rnk_bf, nbk_f)
                plan.op("vector", lambda e: e.tensor_reduce(mkc[:, 0:1], nbk_f[:], mybir.AxisListType.X, OP.max))
                if w == 0:
                    plan.op("vector", lambda e: e.tensor_copy(mkc[:, 2:3], mkc[:, 0:1]))
                else:
                    plan.op("vector", lambda e: e.tensor_tensor(mkc[:, 2:3], mkc[:, 0:1], mkc[:, 1:2], OP.max))
                plan.op("vector", lambda e: e.tensor_copy(mkc[:, 1:2], mkc[:, 0:1]))
                plan.op("vector", lambda e: e.tensor_scalar(b8_bf[:], nbq_f[:], mkc[:, 2:3], 8.0,
                                                            OP.mult, OP.mult), incs=[eV])
                bsums_done = (eV, eV.n)
                for m in list(range(1, 8)) + [0]:
                    qside = m < 4
                    cpair = m % 4
                    selr = selrep[0:8, cpair * 128:cpair * 128 + 128]
                    rn_t = rnq_bf if qside else rnk_bf
                    sc = qscol(l, 0 if qside else 2)
                    ctab = tabs[0] if qside else tabs[2]
                    stab = tabs[1] if qside else tabs[3]
                    dstq = qsb[cpair] if qside else ksb[pw][cpair]
                    plan.op("tensor", lambda e, selr=selr, rn_t=rn_t: e.matmul(
                        bank_ap(0)[:], selr, rn_t[:], start=True, stop=True),
                        waits=[bF[0], bsums_done], incs=[bT[0]])
                    plan.op("vector", lambda e, sc=sc: e.tensor_scalar(a_st[:], bank_ap(0)[:], sc, None, OP.mult),
                            waits=[bT[0]], incs=[bF[0], eV])
                    qsrc = qraw0[:] if m == 0 else bank_ap(m)[:]
                    plan.op("vector", lambda e, qsrc=qsrc, dstq=dstq: e.tensor_tensor(
                        dstq[:], qsrc, a_st[:], OP.mult),
                        incs=[bF[m], eV] if m != 0 else [eV])
                    sw = m if m != 0 else 7
                    plan.op("tensor", lambda e, dstq=dstq, sw=sw: e.matmul(
                        bank_ap(sw)[:], p64[:], dstq[:], start=True, stop=True),
                        waits=[bF[sw], (eV, eV.n)], incs=[bT[sw]])
                    plan.op("vector", lambda e, dstq=dstq, ctab=ctab: e.tensor_tensor(
                        t1_st[:], dstq[:], ctab[:], OP.mult), waits=[tabs_ok])
                    plan.op("vector", lambda e, stab=stab, sw=sw: e.tensor_tensor(
                        t2_st[:, 0:W], bank_ap(sw)[:], stab[:], OP.mult), waits=[bT[sw]], incs=[bF[sw]])
                    plan.op("vector", lambda e, dstq=dstq: e.tensor_tensor(
                        dstq[:], t1_st[:], t2_st[:, 0:W], OP.add), incs=[eV])
                rot_done = (eV, eV.n)
                avm = {}
                e_ok = {}
                for h in range(8):
                    hc, hr = h // 2, (h % 2) * 64
                    plan.op("tensor", lambda e, h=h: e.matmul(
                        bank_ap(4)[:], selh[0:8, h * 128:(h + 1) * 128], b8_bf[:], start=True, stop=True),
                        waits=[bF[4], rot_done], incs=[bT[4]])
                    plan.op("vector", lambda e: e.tensor_copy(brep_bf[:], bank_ap(4)[:]),
                            waits=[bT[4]], incs=[bF[4], eV])
                    brep_ok = (eV, eV.n)
                    ou = 5 + (h % 2)
                    tcis = {t: [ci for ci in range(5) if 4 * (w - 1) + t + ci >= 0] for t in range(4)}
                    scm = {}

                    def emit_scores(t):
                        su = 0 if (t % 2 == 0) else 2
                        scp = pA if su == 0 else pB
                        cis = tcis[t]
                        for ci in cis:
                            kb = 4 * (w - 1) + t + ci
                            wp, cp = (kb // 4) % 2, kb % 4
                            plan.op("tensor", lambda e, ci=ci, scp=scp, wp=wp, cp=cp, hc=hc, hr=hr, t=t: e.matmul(
                                scp[:, ci * 128:(ci + 1) * 128],
                                ksb[wp][hc][hr:hr + 64, cp * 128:(cp + 1) * 128],
                                qsb[hc][hr:hr + 64, t * 128:(t + 1) * 128],
                                start=True, stop=True),
                                waits=[bF[su]] if ci == cis[0] else [],
                                incs=[bT[su]] if ci == cis[-1] else [])
                        scm[t] = (bT[su], bT[su].n)

                    emit_scores(0)
                    emit_scores(1)
                    for t in range(4):
                        su = 0 if (t % 2 == 0) else 2
                        scp = pA if su == 0 else pB
                        cis = tcis[t]
                        ci0, nci = cis[0], len(cis)
                        est = t2_st if t % 2 == 0 else es2_st
                        esv = est[:, 0:nci * 128].rearrange("p (c q) -> p c q", c=nci, q=128)
                        scv = scp[:, ci0 * 128:640].rearrange("p (c q) -> p c q", c=nci, q=128)
                        brv = brep_bf[:, t * 128:(t + 1) * 128].unsqueeze(1).to_broadcast((128, nci, 128))
                        plan.op("vector", lambda e, esv=esv, scv=scv, brv=brv: e.tensor_tensor(
                            esv, scv, brv, OP.subtract),
                            waits=[scm[t], brep_ok] + ([e_ok[(t + 2) % 4]] if (t + 2) % 4 in e_ok else []),
                            incs=[bF[su]])
                        plan.op("vector", lambda e, est=est, nci=nci: e.tensor_tensor(
                            est[:, (nci - 1) * 128:nci * 128], est[:, (nci - 1) * 128:nci * 128],
                            masks[:, 128:256], OP.add))
                        if ci0 == 0:
                            plan.op("vector", lambda e, est=est: e.tensor_tensor(
                                est[:, 0:128], est[:, 0:128], masks[:, 0:128], OP.add))
                        plan.op("vector", None, incs=[eV])
                        ep = e_bf[t]
                        plan.op("scalar", lambda e, ep=ep, est=est, nci=nci: e.activation(
                            ep[:, 0:nci * 128], est[:, 0:nci * 128], AF.Exp),
                            waits=[(eV, eV.n)] + ([avm[t]] if t in avm else []), incs=[eS])
                        e_ok[t] = (eS, eS.n)
                        if t + 2 < 4:
                            emit_scores(t + 2)
                    for t in range(4):
                        cis = tcis[t]
                        ci0, nci = cis[0], len(cis)
                        ep = e_bf[t]
                        for j, ci in enumerate(cis):
                            kb = 4 * (w - 1) + t + ci
                            wp, cp = (kb // 4) % 2, kb % 4
                            plan.op("tensor", lambda e, ou=ou, t=t, wp=wp, cp=cp, h=h, j=j, ci=ci, ci0=ci0, nci=nci, ep=ep: e.matmul(
                                bank_ap(ou)[0:65, t * 128:(t + 1) * 128],
                                vsb[wp][cp][:, h * 65:(h + 1) * 65],
                                ep[:, (ci - ci0) * 128:(ci - ci0 + 1) * 128],
                                start=(j == 0), stop=(j == nci - 1)),
                                waits=[e_ok[t]] + ([bF[ou]] if (t == 0 and j == 0) else []),
                                incs=([bT[ou]] if t == 3 else [eT]) if j == nci - 1 else [])
                        avm[t] = (bT[ou], bT[ou].n) if t == 3 else (eT, eT.n)
                    plan.op("scalar", lambda e, ou=ou: e.activation(
                        rcp_f[:], bank_ap(ou)[64:65, :], AF.Ln, bias=tiny30col[0:1, :]), waits=[bT[ou]])
                    plan.op("scalar", lambda e: e.activation(rcp_f[:], rcp_f[:], AF.Exp, scale=-1.0),
                            incs=[eS])
                    plan.op("tensor", lambda e: e.matmul(bank_ap(7)[0:64, :], onesrow_f[:, 0:64], rcp_f[:],
                                                         start=True, stop=True),
                            waits=[bF[7], (eS, eS.n)], incs=[bT[7]])
                    plan.op("scalar", lambda e, ou=ou: e.activation(o_un[:], bank_ap(ou)[0:64, :], AF.Copy),
                            waits=[(eV, eV.n)], incs=[eS, bF[ou]])
                    plan.op("vector", lambda e, hc=hc, hr=hr: e.tensor_tensor(
                        o_sb[hc][hr:hr + 64, :], o_un[:], bank_ap(7)[0:64, :], OP.mult),
                        waits=[bT[7], (eS, eS.n)], incs=[bF[7], eV])
                o_done = (eV, eV.n)
                womk = {}
                plan.op("sync", lambda e: e.dma_start(
                    out=wob[0][:].rearrange("p (kc m) -> p kc m", kc=4, m=128),
                    in_=wos_d[l, :, 0:128].rearrange("(kc p) m -> p kc m", p=128)),
                    waits=[prev_wo_mark] + (conv_marks['ffn0' if l == 0 else 'ffn1'] if w == 0 else []),
                    incs=[qE[3]], dma=True)
                for mc in range(C):
                    if mc + 1 < C:
                        plan.op("sync", lambda e, mc=mc: e.dma_start(
                            out=wob[(mc + 1) % 2][:].rearrange("p (kc m) -> p kc m", kc=4, m=128),
                            in_=wos_d[l, :, (mc + 1) * 128:(mc + 2) * 128].rearrange(
                                "(kc p) m -> p kc m", p=128)),
                            waits=[womk.get(mc - 1, prev_wo_mark)], incs=[qE[3 + (mc + 1) % 2]], dma=True)
                    wready = (qE[3 + mc % 2], qE[3 + mc % 2].n)
                    zu = mc % 2
                    for kc in range(4):
                        plan.op("tensor", lambda e, mc=mc, kc=kc, zu=zu: e.matmul(
                            bank_ap(zu)[:], wob[mc % 2][:, kc * 128:(kc + 1) * 128], o_sb[kc][:],
                            start=(kc == 0), stop=(kc == 3)),
                            waits=[wready, bF[zu], o_done] if kc == 0 else [],
                            incs=[bT[zu]] if kc == 3 else [])
                    womk[mc] = (bT[zu], bT[zu].n)
                    if mc == C - 1:
                        prev_wo_mark = womk[mc]
                    c0 = w * W + 1
                    plan.op("vector", lambda e, mc=mc, zu=zu, c0=c0: e.tensor_tensor(
                        z[mc][:, c0:c0 + W], z[mc][:, c0:c0 + W], bank_ap(zu)[:], OP.add),
                        waits=[bT[zu]], incs=[bF[zu], eV])
            AR.restore()

        def ph_ffn_p2(l):
            AR.save()
            hb2 = [AR.alloc(128, TT, BF16) for _ in range(C)]
            _sqt = AR.alloc(128, TT, BF16)
            sqt = [_sqt, _sqt]
            rows = AR.alloc(1, 4 * TT, F32)
            tst = AR.alloc(128, TT, F32)
            H = [AR.alloc(128, TT, BF16) for _ in range(M1H)]
            w1b = [AR.alloc(128, 2048, BF16) for _ in range(2)]
            w2b = [AR.alloc(128, M1H * 128, BF16) for _ in range(2)]
            w1mk = {}
            w2mk = {}
            W1s = w1s_d[l]
            W2s = w2s_d[l]
            stash2 = AR.alloc(128, 32, F32)
            for c in range(4, 8):
                plan.op("vector", lambda e, c=c: e.tensor_copy(
                    stash2[:, (c - 4) * 8:(c - 4) * 8 + 8],
                    z[c][:, 0:4096].rearrange("p (t k) -> p t k", k=512)[:, :, 0]), incs=[eV])
            for t in range(NT):
                for c in range(C):
                    if c < 4:
                        c0, c1 = hcols(t, c, TT)
                        plan.op("vector", lambda e, c=c, c0=c0, c1=c1: e.tensor_copy(hb2[c][:], z[c][:, c0:c1]),
                                incs=[eV])
                    else:
                        plan.op("vector", lambda e, c=c, t=t: e.tensor_copy(
                            hb2[c][:, 1:TT], z[c][:, t * TT + 1:t * TT + TT]))
                        plan.op("vector", lambda e, c=c, t=t: e.tensor_copy(
                            hb2[c][:, 0:1], stash2[:, (c - 4) * 8 + t:(c - 4) * 8 + t + 1]), incs=[eV])
                    plan.op("scalar", lambda e, c=c: e.activation(sqt[c % 2][:], hb2[c][:], AF.Square),
                            waits=[(eV, eV.n), (eT, eT.n)], incs=[eS])
                    plan.op("tensor", lambda e, c=c: e.matmul(
                        pC_[0][0:1, :], onescol_bf[:], hb2[c][:], start=(c == 0), stop=(c == C - 1)),
                        waits=[(eV, eV.n)] + ([bF[4]] if c == 0 else []),
                        incs=[bT[4]] if c == C - 1 else [])
                    plan.op("tensor", lambda e, c=c: e.matmul(
                        pC_[1][0:1, :], onescol_bf[:], sqt[c % 2][:], start=(c == 0), stop=(c == C - 1)),
                        waits=[(eS, eS.n)] + ([bF[5]] if c == 0 else []),
                        incs=[bT[5]] if c == C - 1 else [eT])
                mrow = lambda k: rows[0:1, k * TT:(k + 1) * TT] if k < 4 else tst[0:1, 0:TT]
                plan.op("vector", lambda e: e.tensor_scalar(mrow(2), pC_[0][0:1, :], 1.0 / D, None, OP.mult),
                        waits=[bT[4]])
                plan.op("vector", lambda e: e.tensor_scalar(mrow(3), pC_[1][0:1, :], 1.0 / D, None, OP.mult),
                        waits=[bT[5]], incs=[eV])
                plan.op("scalar", lambda e: e.activation(mrow(4), mrow(2), AF.Square), waits=[(eV, eV.n)], incs=[eS])
                plan.op("vector", lambda e: e.tensor_tensor(mrow(3), mrow(3), mrow(4), OP.subtract),
                        waits=[(eS, eS.n)], incs=[eV])
                plan.op("scalar", lambda e: e.activation(mrow(3), mrow(3), AF.Ln, bias=epscol[0:1, :]),
                        waits=[(eV, eV.n)], incs=[bF[4]])
                plan.op("scalar", lambda e: e.activation(mrow(0), mrow(3), AF.Exp, scale=-0.5),
                        incs=[eS, bF[5]])
                plan.op("vector", lambda e: e.tensor_tensor(mrow(1), mrow(2), mrow(0), OP.mult),
                        waits=[(eS, eS.n)], incs=[eV])
                _hl = tst[0:1, 0:TT].bitcast(BF16)
                _hi, _lo = _hl[0:1, 0:TT], _hl[0:1, TT:2 * TT]
                plan.op("vector", lambda e, _hi=_hi: e.tensor_copy(_hi, mrow(0)))
                plan.op("vector", lambda e, _lo=_lo, _hi=_hi: e.tensor_tensor(_lo, mrow(0), _hi, OP.subtract),
                        incs=[eV])
                plan.op("tensor", lambda e, _hi=_hi: e.matmul(bank_ap(2)[:], onesrow_bf[:], _hi, start=True, stop=False),
                        waits=[bF[2], (eV, eV.n)])
                plan.op("tensor", lambda e, _lo=_lo: e.matmul(bank_ap(2)[:], onesrow_bf[:], _lo, start=False, stop=True),
                        incs=[bT[2]])
                _mk0 = (bT[2], bT[2].n)
                plan.op("vector", lambda e, _hi=_hi: e.tensor_copy(_hi, mrow(1)), waits=[_mk0])
                plan.op("vector", lambda e, _lo=_lo, _hi=_hi: e.tensor_tensor(_lo, mrow(1), _hi, OP.subtract),
                        incs=[eV])
                plan.op("tensor", lambda e, _hi=_hi: e.matmul(bank_ap(3)[:], onesrow_bf[:], _hi, start=True, stop=False),
                        waits=[(eV, eV.n)])
                plan.op("tensor", lambda e, _lo=_lo: e.matmul(bank_ap(3)[:], onesrow_bf[:], _lo, start=False, stop=True),
                        incs=[bT[2]])
                for c in range(C):
                    plan.op("vector", lambda e, c=c: e.tensor_tensor(
                        tst[:], hb2[c][:], bank_ap(2)[:], OP.mult), waits=[bT[2]])
                    plan.op("vector", lambda e: e.tensor_tensor(tst[:], tst[:], bank_ap(3)[:], OP.subtract))
                    plan.op("vector", lambda e, c=c: e.tensor_scalar(
                        hb2[c][:], tst[:], lngcol(l, c), lnbcol(l, c), OP.mult, OP.add),
                        incs=([bF[2], eV] if c == C - 1 else [eV]))
                h2_ok = (eV, eV.n)
                for mp in range(M1H):
                    sg, sa = (4, 5) if mp % 2 == 0 else (6, 7)
                    sl = mp % 2
                    for hh in range(2):
                        plan.op("sync", lambda e, mp=mp, sl=sl, hh=hh: e.dma_start(
                            out=w1b[sl][:, hh * 1024:(hh + 1) * 1024].rearrange("p (kc m) -> p kc m", kc=8),
                            in_=W1s[:, hh * HIDP + mp * 128:hh * HIDP + (mp + 1) * 128].rearrange(
                                "(kc p) m -> p kc m", p=128)),
                            waits=([w1mk.get(sl, (eT, 0))] + (conv_marks['w1'] if t == 0 else [])) if hh == 0 else [],
                            incs=[qE[5 + sl]], dma=True)
                    for j, su in ((1, sg), (0, sa)):
                        for kc in range(C):
                            plan.op("tensor", lambda e, kc=kc, sl=sl, su=su, j=j: e.matmul(
                                pC_[su - 4][:], w1b[sl][:, j * 1024 + kc * 128:j * 1024 + (kc + 1) * 128],
                                hb2[kc][:], start=(kc == 0), stop=(kc == C - 1)),
                                waits=[(qE[5 + sl], qE[5 + sl].n), bF[su], h2_ok] if kc == 0 else [],
                                incs=[bT[su]] if kc == C - 1 else [])
                        w1mk[sl] = (bT[su], bT[su].n)
                    plan.op("scalar", lambda e, sg=sg, mp=mp: e.activation(
                        tst[:], pC_[sg - 4][:], AF.Gelu, bias=b1col(l, mp + M1H)),
                        waits=[bT[sg], (eV, eV.n)], incs=[eS])
                    plan.op("vector", lambda e, sa=sa, mp=mp: e.scalar_tensor_tensor(
                        H[mp][:], pC_[sa - 4][:], b1col(l, mp), tst[:], OP.add, OP.mult),
                        waits=[bT[sa], (eS, eS.n)], incs=[bF[sg], bF[sa], eV])
                H_ok = (eV, eV.n)
                for mc in range(C):
                    sl = mc % 2
                    plan.op("sync", lambda e, mc=mc, sl=sl: e.dma_start(
                        out=w2b[sl][:].rearrange("p (kc m) -> p kc m", kc=M1H, m=128),
                        in_=W2s[:, mc * 128:(mc + 1) * 128].rearrange("(kc p) m -> p kc m", p=128)),
                        waits=[w2mk.get(sl, (eT, 0))] + (conv_marks['mlp'] if t == 0 and mc < 2 else []),
                        incs=[qE[7 + sl]], dma=True)
                    zu = sl
                    for kc in range(M1H):
                        plan.op("tensor", lambda e, kc=kc, sl=sl, zu=zu: e.matmul(
                            bank_ap(zu)[:], w2b[sl][:, kc * 128:(kc + 1) * 128], H[kc][:],
                            start=(kc == 0), stop=(kc == M1H - 1)),
                            waits=[(qE[7 + sl], qE[7 + sl].n), bF[zu], H_ok] if kc == 0 else [],
                            incs=[bT[zu]] if kc == M1H - 1 else [])
                    w2mk[sl] = (bT[zu], bT[zu].n)
                    t0 = t * TT + 1
                    plan.op("vector", lambda e, mc=mc, zu=zu, t0=t0: e.scalar_tensor_tensor(
                        z[mc][:, t0:t0 + TT], bank_ap(zu)[:], b2col(l, mc), z[mc][:, t0:t0 + TT],
                        OP.add, OP.add), waits=[bT[zu]], incs=[bF[zu], eV])
            AR.restore()

        def ph_addback(src_d, to_out=False):
            AR.save()
            stg = [AR.alloc(128, TT, F32) for _ in range(4)]
            ostg = [AR.alloc(128, TT, F32) for _ in range(4)] if to_out else None
            mk = {}
            for t in range(NT):
                for c in range(C):
                    sl = (t * C + c) % 4
                    plan.op("gpsimd", lambda e, c=c, t=t, sl=sl: e.dma_start(
                        out=stg[sl][:], in_=src_d[c * 128:(c + 1) * 128, t * TT:(t + 1) * TT]),
                        waits=[mk.get(sl, (eV, 0))], incs=[qD[sl]], dma=True)
                    t0 = t * TT + 1
                    if to_out:
                        plan.op("vector", lambda e, c=c, t0=t0, sl=sl: e.scalar_tensor_tensor(
                            ostg[sl][:], z[c][:, t0:t0 + TT], 0.5, stg[sl][:], OP.mult, OP.add),
                            waits=[(qD[sl], qD[sl].n), mk.get(("o", sl), (qX, 0))], incs=[eV])
                        mk[sl] = (eV, eV.n)
                        plan.op("sync", lambda e, c=c, t=t, sl=sl: e.dma_start(
                            out=out_d[c * 128:(c + 1) * 128, t * TT:(t + 1) * TT], in_=ostg[sl][:]),
                            waits=[mk[sl]], incs=[qE[sl]], dma=True)
                        mk[("o", sl)] = (qE[sl], qE[sl].n)
                    else:
                        plan.op("vector", lambda e, c=c, t0=t0, sl=sl: e.scalar_tensor_tensor(
                            z[c][:, t0:t0 + TT], z[c][:, t0:t0 + TT], 0.5, stg[sl][:], OP.mult, OP.add),
                            waits=[(qD[sl], qD[sl].n)], incs=[eV])
                        mk[sl] = (eV, eV.n)
            if to_out:
                plan.op("sync", None, waits=[(qX, qX.n)] + [(qE[i], qE[i].n) for i in range(4)])
            AR.restore()

        def ph_attn(spill_to=None):
            AR.save()
            theta_sb = AR.alloc(1, D, F32)
            bkv_bf = AR.alloc(1, 2 * D, BF16)
            kv_bf = AR.alloc(128, 16 * 64, BF16)
            plan.op("sync", lambda e: e.dma_start(out=theta_sb[:], in_=theta_d), incs=[qX], dma=True)
            plan.op("gpsimd", lambda e: e.dma_start(out=bkv_bf[:], in_=bkv_d), incs=[qW], dma=True)
            th_ok = [(qX, qX.n), (qW, qW.n)]
            TLP1 = 256

            AR.save()
            _blk1 = AR.alloc(128, 2048, BF16)
            xb1 = [_blk1[:, i * TLP1:(i + 1) * TLP1] for i in range(C)]
            _xq1t = AR.alloc(128, TLP1, BF16)
            xq1 = [_xq1t, _xq1t]
            rows1 = AR.alloc(1, 4 * TLP1, F32)
            tst1 = AR.alloc(128, TLP1, F32)
            y1 = [AR.alloc(128, TLP1, BF16) for _ in range(C)]
            tabs1 = [AR.alloc(128, D, BF16) for _ in range(4)]
            ust = AR.alloc(128, D, F32)
            u2st = AR.alloc(128, D, F32)
            nst = _blk1.bitcast(I32)
            wtmp1 = _blk1.bitcast(F32)
            ksil = u2st
            kp_sb = [AR.alloc(128, 2 * D, BF16) for _ in range(2)]
            _vt1 = AR.alloc(128, D, BF16)
            v_sb = [_vt1, _vt1]
            kv_acc = AR.alloc(128, 16 * 64, F32)
            wt1 = [AR.alloc(128, D, BF16) for _ in range(2)]
            idx1 = AR.alloc(1, TLP1, F32)
            wmk = {}
            idxmk = {}
            for tt in range(N // TLP1):
                t0 = tt * TLP1 + 1
                plan.op("sync", lambda e, tt=tt: e.dma_start(
                    out=idx1[:], in_=idx_d[:, tt * TLP1:(tt + 1) * TLP1]),
                    waits=[idxmk.get(0, (eT, 0))], incs=[qE[2]], dma=True)
                for c in range(C):
                    plan.op("vector", lambda e, c=c, t0=t0: e.tensor_copy(xb1[c][:], z[c][:, t0:t0 + TLP1]),
                            incs=[eV])
                    plan.op("scalar", lambda e, c=c: e.activation(xq1[c % 2][:], xb1[c][:], AF.Square),
                            waits=[(eV, eV.n), (eT, eT.n)], incs=[eS])
                    plan.op("tensor", lambda e, c=c: e.matmul(
                        pC_[0][0:1, 0:TLP1], onescol_bf[:], xb1[c][:], start=(c == 0), stop=(c == C - 1)),
                        waits=[(eV, eV.n)] + ([bF[4]] if c == 0 else []),
                        incs=[bT[4]] if c == C - 1 else [])
                    plan.op("tensor", lambda e, c=c: e.matmul(
                        pC_[1][0:1, 0:TLP1], onescol_bf[:], xq1[c % 2][:], start=(c == 0), stop=(c == C - 1)),
                        waits=[(eS, eS.n)] + ([bF[5]] if c == 0 else []),
                        incs=[bT[5]] if c == C - 1 else [eT])
                mrow1 = lambda k: rows1[0:1, k * TLP1:(k + 1) * TLP1] if k < 4 else tst1[0:1, 0:TLP1]
                plan.op("vector", lambda e: e.tensor_scalar(mrow1(2), pC_[0][0:1, 0:TLP1], 1.0 / D, None, OP.mult),
                        waits=[bT[4]])
                plan.op("vector", lambda e: e.tensor_scalar(mrow1(3), pC_[1][0:1, 0:TLP1], 1.0 / D, None, OP.mult),
                        waits=[bT[5]], incs=[eV])
                plan.op("scalar", lambda e: e.activation(mrow1(4), mrow1(2), AF.Square), waits=[(eV, eV.n)], incs=[eS])
                plan.op("vector", lambda e: e.tensor_tensor(mrow1(3), mrow1(3), mrow1(4), OP.subtract),
                        waits=[(eS, eS.n)], incs=[eV])
                plan.op("scalar", lambda e: e.activation(mrow1(3), mrow1(3), AF.Ln, bias=epscol[0:1, :]),
                        waits=[(eV, eV.n)], incs=[bF[4]])
                plan.op("scalar", lambda e: e.activation(mrow1(0), mrow1(3), AF.Exp, scale=-0.5),
                        incs=[eS, bF[5]])
                plan.op("vector", lambda e: e.tensor_tensor(mrow1(1), mrow1(2), mrow1(0), OP.mult),
                        waits=[(eS, eS.n)], incs=[eV])
                _hl = tst1[0:1, 0:TLP1].bitcast(BF16)
                _hi, _lo = _hl[0:1, 0:TLP1], _hl[0:1, TLP1:2 * TLP1]
                plan.op("vector", lambda e, _hi=_hi: e.tensor_copy(_hi, mrow1(0)))
                plan.op("vector", lambda e, _lo=_lo, _hi=_hi: e.tensor_tensor(_lo, mrow1(0), _hi, OP.subtract),
                        incs=[eV])
                plan.op("tensor", lambda e, _hi=_hi: e.matmul(pC_[0][:, 0:TLP1], onesrow_bf[:], _hi, start=True, stop=False),
                        waits=[bF[4], (eV, eV.n)])
                plan.op("tensor", lambda e, _lo=_lo: e.matmul(pC_[0][:, 0:TLP1], onesrow_bf[:], _lo, start=False, stop=True),
                        incs=[bT[4]])
                _mk0 = (bT[4], bT[4].n)
                plan.op("vector", lambda e, _hi=_hi: e.tensor_copy(_hi, mrow1(1)), waits=[_mk0])
                plan.op("vector", lambda e, _lo=_lo, _hi=_hi: e.tensor_tensor(_lo, mrow1(1), _hi, OP.subtract),
                        incs=[eV])
                plan.op("tensor", lambda e, _hi=_hi: e.matmul(pC_[1][:, 0:TLP1], onesrow_bf[:], _hi, start=True, stop=False),
                        waits=[bF[5], (eV, eV.n)])
                plan.op("tensor", lambda e, _lo=_lo: e.matmul(pC_[1][:, 0:TLP1], onesrow_bf[:], _lo, start=False, stop=True),
                        incs=[bT[5]])
                for c in range(C):
                    plan.op("vector", lambda e, c=c, t0=t0: e.tensor_tensor(
                        tst1[:], z[c][:, t0:t0 + TLP1], pC_[0][:, 0:TLP1], OP.mult), waits=[bT[4], bT[5]])
                    plan.op("vector", lambda e, c=c: e.tensor_tensor(
                        y1[c][:], tst1[:], pC_[1][:, 0:TLP1], OP.subtract),
                        incs=([bF[4], bF[5], eV] if c == C - 1 else [eV]))
                y_ok = (eV, eV.n)
                for s in range(2):
                    plan.op("tensor", lambda e, s=s: e.matmul(
                        pC_[0][:], idx1[:, s * 128:(s + 1) * 128], theta_sb[:, 0:512], start=True, stop=True),
                        waits=[(qE[2], qE[2].n), bF[4]] + (th_ok if tt == 0 and s == 0 else []),
                        incs=[bT[4]])
                    plan.op("tensor", lambda e, s=s: e.matmul(
                        pC_[1][:], idx1[:, s * 128:(s + 1) * 128], theta_sb[:, 512:1024], start=True, stop=True),
                        waits=[bF[5]], incs=[bT[5]])
                    idxmk[0] = (bT[5], bT[5].n)
                    plan.op("vector", lambda e: e.tensor_scalar(ust[:, 0:512], pC_[0][:], INV2PI, None, OP.mult),
                            waits=[bT[4], (eS, eS.n)])
                    plan.op("vector", lambda e: e.tensor_scalar(u2st[:, 0:512], pC_[0][:], INV2PI, 0.25, OP.mult, OP.add))
                    plan.op("vector", lambda e: e.tensor_scalar(ust[:, 512:1024], pC_[1][:], INV2PI, None, OP.mult),
                            waits=[bT[5]])
                    plan.op("vector", lambda e: e.tensor_scalar(u2st[:, 512:1024], pC_[1][:], INV2PI, 0.25, OP.mult, OP.add),
                            incs=[eV])
                    plan.op("scalar", lambda e: e.copy(nst[:], ust[:]), waits=[(eV, eV.n)], incs=[eS])
                    plan.op("vector", lambda e: e.scalar_tensor_tensor(
                        ust[:, 0:512], nst[:, 0:512], -TWOPI, pC_[0][:], OP.mult, OP.add), waits=[(eS, eS.n)])
                    plan.op("vector", lambda e: e.scalar_tensor_tensor(
                        ust[:, 512:1024], nst[:, 512:1024], -TWOPI, pC_[1][:], OP.mult, OP.add))
                    plan.op("vector", None, incs=[eV])
                    plan.op("vector", lambda e: e.tensor_scalar(wtmp1[:], ust[:], PI_, TWOPI, OP.is_gt, OP.mult),
                            waits=[(eV, eV.n)])
                    plan.op("vector", lambda e: e.tensor_tensor(ust[:], ust[:], wtmp1[:], OP.subtract))
                    plan.op("vector", lambda e: e.tensor_scalar(wtmp1[:], ust[:], -PI_, TWOPI, OP.is_lt, OP.mult))
                    plan.op("vector", lambda e: e.tensor_tensor(ust[:], ust[:], wtmp1[:], OP.add), incs=[eV])
                    plan.op("scalar", lambda e, s=s: e.activation(tabs1[2 * s + 1][:], ust[:], AF.Sin),
                            waits=[(eV, eV.n)], incs=[eS])
                    plan.op("scalar", lambda e: e.copy(nst[:], u2st[:]), waits=[(eV, eV.n)], incs=[eS])
                    plan.op("vector", lambda e: e.scalar_tensor_tensor(
                        u2st[:, 0:512], nst[:, 0:512], -TWOPI, pC_[0][:], OP.mult, OP.add),
                        waits=[(eS, eS.n)], incs=[bF[4]])
                    plan.op("vector", lambda e: e.scalar_tensor_tensor(
                        u2st[:, 512:1024], nst[:, 512:1024], -TWOPI, pC_[1][:], OP.mult, OP.add), incs=[bF[5]])
                    plan.op("vector", None, incs=[eV])
                    plan.op("vector", lambda e: e.tensor_scalar(wtmp1[:], u2st[:], PI_ / 2, TWOPI, OP.is_gt, OP.mult),
                            waits=[(eV, eV.n)])
                    plan.op("vector", lambda e: e.tensor_tensor(u2st[:], u2st[:], wtmp1[:], OP.subtract))
                    plan.op("vector", lambda e: e.tensor_scalar(wtmp1[:], u2st[:], -1.5 * PI_, TWOPI, OP.is_lt, OP.mult))
                    plan.op("vector", lambda e: e.tensor_tensor(u2st[:], u2st[:], wtmp1[:], OP.add), incs=[eV])
                    plan.op("scalar", lambda e, s=s: e.activation(tabs1[2 * s][:], u2st[:], AF.Sin,
                                                                 bias=halfpicol[:]),
                            waits=[(eV, eV.n)], incs=[eS])
                tabs_ok1 = (eS, eS.n)
                kmk = {}
                plan.op("sync", lambda e: e.dma_start(out=wt1[0][:], in_=wquTs_d[0:128, D:2 * D]),
                        waits=[wmk.get(0, (eT, 0))] + (conv_done if tt == 0 else []), incs=[qE[9]], dma=True)
                for kc in range(C):
                    if kc + 1 < C:
                        plan.op("sync", lambda e, kc=kc: e.dma_start(
                            out=wt1[(kc + 1) % 2][:], in_=wquTs_d[(kc + 1) * 128:(kc + 2) * 128, D:2 * D]),
                            waits=[kmk.get(kc - 1, wmk.get(1, (eT, 0)))], incs=[qE[9 + (kc + 1) % 2]], dma=True)
                    for s in range(2):
                        su = 2 * s
                        for half in range(2):
                            plan.op("tensor", lambda e, kc=kc, s=s, half=half, su=su: e.matmul(
                                bank_ap(su + half)[:], y1[kc][:, s * 128:(s + 1) * 128],
                                wt1[kc % 2][:, half * 512:(half + 1) * 512],
                                start=(kc == 0), stop=False),
                                waits=[(qE[9 + kc % 2], qE[9 + kc % 2].n)] + ([bF[su + half]] if kc == 0 else []),
                                incs=[eT] if (s == 1 and half == 1) else [])
                    kmk[kc] = (eT, eT.n)
                for s in range(2):
                    su = 2 * s
                    for half in range(2):
                        plan.op("tensor", lambda e, s=s, half=half, su=su: e.matmul(
                            bank_ap(su + half)[:], onesrow_bf[:, s * 0:128][0:1, :],
                            bkv_bf[:, half * 512:(half + 1) * 512], start=False, stop=True),
                            incs=[bT[su + half]])
                for s in range(2):
                    su = 2 * s
                    plan.op("scalar", lambda e, s=s, su=su: e.activation(
                        ksil[:, 0:512], bank_ap(su)[:], AF.Silu), waits=[bT[su], (eV, eV.n)], incs=[eS])
                    plan.op("scalar", lambda e, s=s, su=su: e.activation(
                        ksil[:, 512:1024], bank_ap(su + 1)[:], AF.Silu), waits=[bT[su + 1]],
                        incs=[eS, bF[su], bF[su + 1]])
                    kv3 = kp_sb[s][:].rearrange("p (h two d) -> p (h two) d", two=2, d=64)
                    plan.op("vector", lambda e, s=s, kv3=kv3: e.tensor_tensor(
                        kv3.rearrange("p (h two) d -> p h two d", two=2)[:, :, 0, :],
                        ksil[:].rearrange("p (h d) -> p h d", d=64),
                        tabs1[2 * s][:].rearrange("p (h d) -> p h d", d=64), OP.mult),
                        waits=[(eS, eS.n), tabs_ok1])
                    plan.op("vector", lambda e, s=s, kv3=kv3: e.tensor_tensor(
                        kv3.rearrange("p (h two) d -> p h two d", two=2)[:, :, 1, :],
                        ksil[:].rearrange("p (h d) -> p h d", d=64),
                        tabs1[2 * s + 1][:].rearrange("p (h d) -> p h d", d=64), OP.mult), incs=[eV])
                kp_ok = (eV, eV.n)
                vmk = {}
                plan.op("sync", lambda e: e.dma_start(out=wt1[0][:], in_=wquTs_d[0:128, 2 * D:3 * D]),
                        waits=[kmk[C - 1]], incs=[qE[9]], dma=True)
                for kc in range(C):
                    if kc + 1 < C:
                        plan.op("sync", lambda e, kc=kc: e.dma_start(
                            out=wt1[(kc + 1) % 2][:], in_=wquTs_d[(kc + 1) * 128:(kc + 2) * 128, 2 * D:3 * D]),
                            waits=[vmk.get(kc - 1, kmk[C - 1])], incs=[qE[9 + (kc + 1) % 2]], dma=True)
                    for s in range(2):
                        su = 2 * s
                        for half in range(2):
                            plan.op("tensor", lambda e, kc=kc, s=s, half=half, su=su: e.matmul(
                                bank_ap(su + half)[:], y1[kc][:, s * 128:(s + 1) * 128],
                                wt1[kc % 2][:, half * 512:(half + 1) * 512],
                                start=(kc == 0), stop=False),
                                waits=[(qE[9 + kc % 2], qE[9 + kc % 2].n)] + ([bF[su + half]] if kc == 0 else []),
                                incs=[eT] if (s == 1 and half == 1) else [])
                    vmk[kc] = (eT, eT.n)
                wmk[0] = vmk[C - 1]
                wmk[1] = vmk[C - 1]
                for s in range(2):
                    su = 2 * s
                    for half in range(2):
                        plan.op("tensor", lambda e, s=s, half=half, su=su: e.matmul(
                            bank_ap(su + half)[:], onesrow_bf[0:1, :],
                            bkv_bf[:, D + half * 512:D + (half + 1) * 512], start=False, stop=True),
                            incs=[bT[su + half]])
                for s in range(2):
                    su = 2 * s
                    plan.op("vector", lambda e, s=s, su=su: e.tensor_copy(v_sb[s][:, 0:512], bank_ap(su)[:]),
                            waits=[bT[su], bT[6], bT[7]], incs=[bF[su]])
                    plan.op("vector", lambda e, s=s, su=su: e.tensor_copy(v_sb[s][:, 512:1024], bank_ap(su + 1)[:]),
                            waits=[bT[su + 1]], incs=[bF[su + 1], eV])
                    v_ok = (eV, eV.n)
                    for h in range(16):
                        un = 6 + h // 8
                        plan.op("tensor", lambda e, s=s, h=h, un=un: e.matmul(
                            bank_ap(un)[:, (h % 8) * 64:(h % 8) * 64 + 64],
                            kp_sb[s][:, h * 128:(h + 1) * 128], v_sb[s][:, h * 64:(h + 1) * 64],
                            start=True, stop=True),
                            waits=([kp_ok, v_ok, bF[6], bF[7]] if h == 0 else []),
                            incs=[bT[6]] if h == 7 else ([bT[7]] if h == 15 else []))
                    for un in (6, 7):
                        if tt == 0 and s == 0:
                            plan.op("vector", lambda e, un=un: e.tensor_copy(
                                kv_acc[:, (un - 6) * 512:(un - 5) * 512], bank_ap(un)[:]),
                                waits=[bT[un]], incs=[bF[un]])
                        else:
                            plan.op("vector", lambda e, un=un: e.tensor_tensor(
                                kv_acc[:, (un - 6) * 512:(un - 5) * 512],
                                kv_acc[:, (un - 6) * 512:(un - 5) * 512], bank_ap(un)[:], OP.add),
                                waits=[bT[un]], incs=[bF[un]])
                    plan.op("vector", None, incs=[eV])
            plan.op("vector", lambda e: e.tensor_copy(kv_bf[:], kv_acc[:]), incs=[eV])
            AR.restore()
            barrier()

            AR.save()
            xb2 = [AR.alloc(128, TT, BF16) for _ in range(C)]
            _xq2t = AR.alloc(128, TT, BF16)
            xq2 = [_xq2t, _xq2t]
            rows2 = AR.alloc(1, 4 * TT, F32)
            tst2 = AR.alloc(128, TT, F32)
            y2 = [AR.alloc(128, TT, BF16) for _ in range(C)]
            qsil = [AR.alloc(128, TT, BF16) for _ in range(C)]
            ctab2 = AR.alloc(128, TT, BF16)
            stab2 = AR.alloc(128, TT, BF16)
            qful = [AR.alloc(128, TT, BF16) for _ in range(2)]
            ust2 = tst2
            u2st2 = AR.alloc(128, TT, F32)
            nst2 = AR.alloc(128, TT, I32)
            wtmp2 = nst2.bitcast(F32)
            wt2 = [AR.alloc(128, D, BF16) for _ in range(2)]
            idx2 = AR.alloc(1, TT, F32)
            wmk2 = {}
            idxmk2 = {}
            for t in range(NT):
                t0 = t * TT + 1
                plan.op("sync", lambda e, t=t: e.dma_start(
                    out=idx2[:], in_=idx_d[:, t * TT:(t + 1) * TT]),
                    waits=[idxmk2.get(0, (eT, 0))], incs=[qE[2]], dma=True)
                for c in range(C):
                    plan.op("vector", lambda e, c=c, t0=t0: e.tensor_copy(xb2[c][:], z[c][:, t0:t0 + TT]),
                            incs=[eV])
                    plan.op("scalar", lambda e, c=c: e.activation(xq2[c % 2][:], xb2[c][:], AF.Square),
                            waits=[(eV, eV.n), (eT, eT.n)], incs=[eS])
                    plan.op("tensor", lambda e, c=c: e.matmul(
                        pC_[0][0:1, :], onescol_bf[:], xb2[c][:], start=(c == 0), stop=(c == C - 1)),
                        waits=[(eV, eV.n)] + ([bF[4]] if c == 0 else []),
                        incs=[bT[4]] if c == C - 1 else [])
                    plan.op("tensor", lambda e, c=c: e.matmul(
                        pC_[1][0:1, :], onescol_bf[:], xq2[c % 2][:], start=(c == 0), stop=(c == C - 1)),
                        waits=[(eS, eS.n)] + ([bF[5]] if c == 0 else []),
                        incs=[bT[5]] if c == C - 1 else [eT])
                mrow = lambda k: rows2[0:1, k * TT:(k + 1) * TT] if k < 4 else tst2[0:1, 0:TT]
                plan.op("vector", lambda e: e.tensor_scalar(mrow(2), pC_[0][0:1, :], 1.0 / D, None, OP.mult),
                        waits=[bT[4]])
                plan.op("vector", lambda e: e.tensor_scalar(mrow(3), pC_[1][0:1, :], 1.0 / D, None, OP.mult),
                        waits=[bT[5]], incs=[eV])
                plan.op("scalar", lambda e: e.activation(mrow(4), mrow(2), AF.Square), waits=[(eV, eV.n)], incs=[eS])
                plan.op("vector", lambda e: e.tensor_tensor(mrow(3), mrow(3), mrow(4), OP.subtract),
                        waits=[(eS, eS.n)], incs=[eV])
                plan.op("scalar", lambda e: e.activation(mrow(3), mrow(3), AF.Ln, bias=epscol[0:1, :]),
                        waits=[(eV, eV.n)], incs=[bF[4]])
                plan.op("scalar", lambda e: e.activation(mrow(0), mrow(3), AF.Exp, scale=-0.5),
                        incs=[eS, bF[5]])
                plan.op("vector", lambda e: e.tensor_tensor(mrow(1), mrow(2), mrow(0), OP.mult),
                        waits=[(eS, eS.n)], incs=[eV])
                _hl = tst2[0:1, 0:TT].bitcast(BF16)
                _hi, _lo = _hl[0:1, 0:TT], _hl[0:1, TT:2 * TT]
                plan.op("vector", lambda e, _hi=_hi: e.tensor_copy(_hi, mrow(0)))
                plan.op("vector", lambda e, _lo=_lo, _hi=_hi: e.tensor_tensor(_lo, mrow(0), _hi, OP.subtract),
                        incs=[eV])
                plan.op("tensor", lambda e, _hi=_hi: e.matmul(pC_[0][:], onesrow_bf[:], _hi, start=True, stop=False),
                        waits=[bF[4], (eV, eV.n)])
                plan.op("tensor", lambda e, _lo=_lo: e.matmul(pC_[0][:], onesrow_bf[:], _lo, start=False, stop=True),
                        incs=[bT[4]])
                _mk0 = (bT[4], bT[4].n)
                plan.op("vector", lambda e, _hi=_hi: e.tensor_copy(_hi, mrow(1)), waits=[_mk0])
                plan.op("vector", lambda e, _lo=_lo, _hi=_hi: e.tensor_tensor(_lo, mrow(1), _hi, OP.subtract),
                        incs=[eV])
                plan.op("tensor", lambda e, _hi=_hi: e.matmul(pC_[1][:], onesrow_bf[:], _hi, start=True, stop=False),
                        waits=[bF[5], (eV, eV.n)])
                plan.op("tensor", lambda e, _lo=_lo: e.matmul(pC_[1][:], onesrow_bf[:], _lo, start=False, stop=True),
                        incs=[bT[5]])
                for c in range(C):
                    plan.op("vector", lambda e, c=c, t0=t0: e.tensor_tensor(
                        tst2[:], z[c][:, t0:t0 + TT], pC_[0][:], OP.mult), waits=[bT[4], bT[5]])
                    plan.op("vector", lambda e, c=c: e.tensor_tensor(
                        y2[c][:], tst2[:], pC_[1][:], OP.subtract),
                        incs=([bF[4], bF[5], eV] if c == C - 1 else [eV]))
                y_ok = (eV, eV.n)
                qmk = {}
                plan.op("sync", lambda e: e.dma_start(out=wt2[0][:], in_=wquTs_d[0:128, 0:D]),
                        waits=[wmk2.get(0, (eT, 0))], incs=[qE[9]], dma=True)
                for kc in range(C):
                    if kc + 1 < C:
                        plan.op("sync", lambda e, kc=kc: e.dma_start(
                            out=wt2[(kc + 1) % 2][:], in_=wquTs_d[(kc + 1) * 128:(kc + 2) * 128, 0:D]),
                            waits=[qmk.get(kc - 1, wmk2.get(1, (eT, 0)))], incs=[qE[9 + (kc + 1) % 2]], dma=True)
                    for m in range(8):
                        plan.op("tensor", lambda e, kc=kc, m=m: e.matmul(
                            bank_ap(m)[:], wt2[kc % 2][:, m * 128:(m + 1) * 128], y2[kc][:],
                            start=(kc == 0), stop=(kc == C - 1)),
                            waits=[(qE[9 + kc % 2], qE[9 + kc % 2].n), y_ok] + ([bF[m]] if kc == 0 else [])
                            if kc == 0 else ([(qE[9 + kc % 2], qE[9 + kc % 2].n)] if m == 0 else []),
                            incs=[bT[m]] if kc == C - 1 else ([eT] if m == 7 else []))
                    qmk[kc] = (bT[7], bT[7].n) if kc == C - 1 else (eT, eT.n)
                for m in range(8):
                    plan.op("scalar", lambda e, m=m: e.activation(
                        qsil[m][:], bank_ap(m)[:], AF.Silu, bias=acol(0, m)),
                        waits=[bT[m], (eV, eV.n)], incs=[eS, bF[m]])
                qsil_ok = (eS, eS.n)
                for hp in range(8):
                    plan.op("tensor", lambda e, hp=hp: e.matmul(
                        pC_[0][:], theta_sb[:, hp * 128:(hp + 1) * 128],
                        idx2[:], start=True, stop=True), waits=[bF[4], (qE[2], qE[2].n)],
                        incs=[bT[4]])
                    idxmk2[0] = (bT[4], bT[4].n)
                    plan.op("vector", lambda e: e.tensor_scalar(ust2[:], pC_[0][:], INV2PI, None, OP.mult),
                            waits=[bT[4], (eS, eS.n)])
                    plan.op("vector", lambda e: e.tensor_scalar(u2st2[:], pC_[0][:], INV2PI, 0.25, OP.mult, OP.add),
                            incs=[eV])
                    plan.op("scalar", lambda e: e.copy(nst2[:], ust2[:]), waits=[(eV, eV.n)], incs=[eS])
                    plan.op("vector", lambda e: e.scalar_tensor_tensor(
                        ust2[:], nst2[:], -TWOPI, pC_[0][:], OP.mult, OP.add), waits=[(eS, eS.n)])
                    plan.op("vector", None, incs=[eV])
                    plan.op("vector", lambda e: e.tensor_scalar(wtmp2[:], ust2[:], PI_, TWOPI, OP.is_gt, OP.mult),
                            waits=[(eV, eV.n)])
                    plan.op("vector", lambda e: e.tensor_tensor(ust2[:], ust2[:], wtmp2[:], OP.subtract))
                    plan.op("vector", lambda e: e.tensor_scalar(wtmp2[:], ust2[:], -PI_, TWOPI, OP.is_lt, OP.mult))
                    plan.op("vector", lambda e: e.tensor_tensor(ust2[:], ust2[:], wtmp2[:], OP.add), incs=[eV])
                    plan.op("scalar", lambda e: e.activation(stab2[:], ust2[:], AF.Sin),
                            waits=[(eV, eV.n)], incs=[eS])
                    plan.op("scalar", lambda e: e.copy(nst2[:], u2st2[:]), waits=[(eV, eV.n)], incs=[eS])
                    plan.op("vector", lambda e: e.scalar_tensor_tensor(
                        u2st2[:], nst2[:], -TWOPI, pC_[0][:], OP.mult, OP.add),
                        waits=[(eS, eS.n)], incs=[bF[4]])
                    plan.op("vector", None, incs=[eV])
                    plan.op("vector", lambda e: e.tensor_scalar(wtmp2[:], u2st2[:], PI_ / 2, TWOPI, OP.is_gt, OP.mult),
                            waits=[(eV, eV.n)])
                    plan.op("vector", lambda e: e.tensor_tensor(u2st2[:], u2st2[:], wtmp2[:], OP.subtract))
                    plan.op("vector", lambda e: e.tensor_scalar(wtmp2[:], u2st2[:], -1.5 * PI_, TWOPI, OP.is_lt, OP.mult))
                    plan.op("vector", lambda e: e.tensor_tensor(u2st2[:], u2st2[:], wtmp2[:], OP.add), incs=[eV])
                    plan.op("scalar", lambda e: e.activation(ctab2[:], u2st2[:], AF.Sin,
                                                            bias=halfpicol[:]),
                            waits=[(eV, eV.n)], incs=[eS])
                    for hh in range(2):
                        h = 2 * hp + hh
                        hr = hh * 64
                        qf = qful[hh]
                        plan.op("vector", lambda e, hr=hr, hp=hp, qf=qf: e.tensor_tensor(
                            qf[0:64, :], qsil[hp][hr:hr + 64, :], ctab2[hr:hr + 64, :], OP.mult),
                            waits=[(eS, eS.n), qsil_ok, bT[6 - 1] if False else (eT, 0)])
                        plan.op("vector", lambda e, hr=hr, hp=hp, qf=qf: e.tensor_tensor(
                            qf[64:128, :], qsil[hp][hr:hr + 64, :], stab2[hr:hr + 64, :], OP.mult),
                            incs=[eV])
                        ou = 5 + hh
                        plan.op("tensor", lambda e, h=h, qf=qf, ou=ou: e.matmul(
                            bank_ap(ou)[(h % 2) * 0:64, :] if False else bank_ap(ou)[0:64, :],
                            kv_bf[:, h * 64:(h + 1) * 64], qf[:], start=True, stop=True),
                            waits=[bF[ou], (eV, eV.n)], incs=[bT[ou]])
                        plan.op("scalar", lambda e, hp=hp, hr=hr, ou=ou: e.activation(
                            xb2[hp][hr:hr + 64, :], bank_ap(ou)[0:64, :], AF.Copy),
                            waits=[bT[ou]], incs=[eS, bF[ou]])
                o_ok = (eS, eS.n)
                for c in range(C):
                    plan.op("scalar", lambda e, c=c: e.activation(xq2[c % 2][:], xb2[c][:], AF.Square),
                            waits=[(eT, eT.n)], incs=[eS])
                    plan.op("tensor", lambda e, c=c: e.matmul(
                        pC_[0][0:1, :], onescol_bf[:], xb2[c][:], start=(c == 0), stop=(c == C - 1)),
                        waits=[o_ok] + ([bF[4]] if c == 0 else []),
                        incs=[bT[4]] if c == C - 1 else [])
                    plan.op("tensor", lambda e, c=c: e.matmul(
                        pC_[1][0:1, :], onescol_bf[:], xq2[c % 2][:], start=(c == 0), stop=(c == C - 1)),
                        waits=[(eS, eS.n)] + ([bF[5]] if c == 0 else []),
                        incs=[bT[5]] if c == C - 1 else [eT])
                plan.op("vector", lambda e: e.tensor_scalar(mrow(2), pC_[0][0:1, :], 1.0 / D, None, OP.mult),
                        waits=[bT[4]])
                plan.op("vector", lambda e: e.tensor_scalar(mrow(3), pC_[1][0:1, :], 1.0 / D, None, OP.mult),
                        waits=[bT[5]], incs=[eV])
                plan.op("scalar", lambda e: e.activation(mrow(4), mrow(2), AF.Square), waits=[(eV, eV.n)], incs=[eS])
                plan.op("vector", lambda e: e.tensor_tensor(mrow(3), mrow(3), mrow(4), OP.subtract),
                        waits=[(eS, eS.n)], incs=[eV])
                plan.op("scalar", lambda e: e.activation(mrow(3), mrow(3), AF.Ln, bias=epscol[0:1, :]),
                        waits=[(eV, eV.n)], incs=[bF[4]])
                plan.op("scalar", lambda e: e.activation(mrow(0), mrow(3), AF.Exp, scale=-0.5),
                        incs=[eS, bF[5]])
                plan.op("vector", lambda e: e.tensor_tensor(mrow(1), mrow(2), mrow(0), OP.mult),
                        waits=[(eS, eS.n)], incs=[eV])
                _hl = tst2[0:1, 0:TT].bitcast(BF16)
                _hi, _lo = _hl[0:1, 0:TT], _hl[0:1, TT:2 * TT]
                plan.op("vector", lambda e, _hi=_hi: e.tensor_copy(_hi, mrow(0)))
                plan.op("vector", lambda e, _lo=_lo, _hi=_hi: e.tensor_tensor(_lo, mrow(0), _hi, OP.subtract),
                        incs=[eV])
                plan.op("tensor", lambda e, _hi=_hi: e.matmul(pC_[0][:], onesrow_bf[:], _hi, start=True, stop=False),
                        waits=[bF[4], (eV, eV.n)])
                plan.op("tensor", lambda e, _lo=_lo: e.matmul(pC_[0][:], onesrow_bf[:], _lo, start=False, stop=True),
                        incs=[bT[4]])
                _mk0 = (bT[4], bT[4].n)
                plan.op("vector", lambda e, _hi=_hi: e.tensor_copy(_hi, mrow(1)), waits=[_mk0])
                plan.op("vector", lambda e, _lo=_lo, _hi=_hi: e.tensor_tensor(_lo, mrow(1), _hi, OP.subtract),
                        incs=[eV])
                plan.op("tensor", lambda e, _hi=_hi: e.matmul(pC_[1][:], onesrow_bf[:], _hi, start=True, stop=False),
                        waits=[bF[5], (eV, eV.n)])
                plan.op("tensor", lambda e, _lo=_lo: e.matmul(pC_[1][:], onesrow_bf[:], _lo, start=False, stop=True),
                        incs=[bT[5]])
                for c in range(C):
                    plan.op("vector", lambda e, c=c: e.tensor_tensor(
                        tst2[:], xb2[c][:], pC_[0][:], OP.mult), waits=[bT[4], bT[5]])
                    plan.op("vector", lambda e: e.tensor_tensor(tst2[:], tst2[:], pC_[1][:], OP.subtract))
                    plan.op("vector", lambda e, c=c: e.tensor_scalar(
                        xb2[c][:], tst2[:], acol(3, c), alnb[:, c:c + 1], OP.mult, OP.add),
                        incs=([bF[4], bF[5], eV] if c == C - 1 else [eV]))
                op_ok = (eV, eV.n)
                umk = {}
                plan.op("sync", lambda e: e.dma_start(out=wt2[0][:], in_=wquTs_d[0:128, 3 * D:4 * D]),
                        waits=[qmk[C - 1]], incs=[qE[9]], dma=True)
                for kc in range(C):
                    if kc + 1 < C:
                        plan.op("sync", lambda e, kc=kc: e.dma_start(
                            out=wt2[(kc + 1) % 2][:], in_=wquTs_d[(kc + 1) * 128:(kc + 2) * 128, 3 * D:4 * D]),
                            waits=[umk.get(kc - 1, qmk[C - 1])], incs=[qE[9 + (kc + 1) % 2]], dma=True)
                    for m in range(8):
                        plan.op("tensor", lambda e, kc=kc, m=m: e.matmul(
                            bank_ap(m)[:], wt2[kc % 2][:, m * 128:(m + 1) * 128], y2[kc][:],
                            start=(kc == 0), stop=(kc == C - 1)),
                            waits=([(qE[9 + kc % 2], qE[9 + kc % 2].n)] + ([bF[m]] if kc == 0 else []))
                            if (kc == 0 or m == 0) else [],
                            incs=[bT[m]] if kc == C - 1 else ([eT] if m == 7 else []))
                    umk[kc] = (bT[7], bT[7].n) if kc == C - 1 else (eT, eT.n)
                wmk2[0] = umk[C - 1]
                wmk2[1] = umk[C - 1]
                for m in range(8):
                    plan.op("vector", lambda e, m=m: e.scalar_tensor_tensor(
                        xb2[m][:], bank_ap(m)[:], acol(1, m), xb2[m][:], OP.add, OP.mult),
                        waits=[bT[m], op_ok], incs=[bF[m], eV])
                o2_ok = (eV, eV.n)
                wo2 = {}
                plan.op("sync", lambda e: e.dma_start(out=wt2[0][:], in_=woutTs_d[0:128, :]),
                        waits=[umk[C - 1]], incs=[qE[9]], dma=True)
                for kc in range(C):
                    if kc + 1 < C:
                        plan.op("sync", lambda e, kc=kc: e.dma_start(
                            out=wt2[(kc + 1) % 2][:], in_=woutTs_d[(kc + 1) * 128:(kc + 2) * 128, :]),
                            waits=[wo2.get(kc - 1, umk[C - 1])], incs=[qE[9 + (kc + 1) % 2]], dma=True)
                    for mc in range(8):
                        plan.op("tensor", lambda e, kc=kc, mc=mc: e.matmul(
                            bank_ap(mc)[:], wt2[kc % 2][:, mc * 128:(mc + 1) * 128], xb2[kc][:],
                            start=(kc == 0), stop=(kc == C - 1)),
                            waits=([(qE[9 + kc % 2], qE[9 + kc % 2].n), o2_ok] + ([bF[mc]] if kc == 0 else []))
                            if (kc == 0 or mc == 0) else [],
                            incs=[bT[mc]] if kc == C - 1 else ([eT] if mc == 7 else []))
                    wo2[kc] = (bT[7], bT[7].n) if kc == C - 1 else (eT, eT.n)
                wmk2[0] = wo2[C - 1]
                wmk2[1] = wo2[C - 1]
                addmk = {}
                for mc in range(8):
                    plan.op("vector", lambda e, mc=mc, t0=t0: e.scalar_tensor_tensor(
                        z[mc][:, t0:t0 + TT], bank_ap(mc)[:], acol(2, mc), z[mc][:, t0:t0 + TT],
                        OP.add, OP.add), waits=[bT[mc]], incs=[bF[mc], eV])
                    addmk[mc] = (eV, eV.n)
                if spill_to is not None:
                    for c2 in range(C):
                        plan.op("sync", lambda e, c2=c2, t=t, t0=t0: e.dma_start(
                            out=spill_to[c2 * 128:(c2 + 1) * 128, t * TT:(t + 1) * TT],
                            in_=z[c2][:, t0:t0 + TT]),
                            waits=[addmk[c2]], incs=[qX], dma=True)
            AR.restore()
            AR.restore()

        ph_ln0()
        barrier()
        if stop_after == "ln0":
            dump_stream_and_finish()
        elif stop_after == "ffn_l0_attn":
            ph_ffn_p1(0)
            barrier()
            dump_stream_and_finish()
        elif stop_after == "ffn_l0":
            ph_ffn_p1(0)
            barrier()
            ph_ffn_p2(0)
            barrier()
            dump_stream_and_finish()
        elif stop_after in ("ffn_l1", "ffn1"):
            for l in range(2):
                ph_ffn_p1(l)
                barrier()
                ph_ffn_p2(l)
                barrier()
            if stop_after == "ffn1":
                ph_addback(x_fm)
                barrier()
            dump_stream_and_finish()
        elif stop_after in ("attn", "full"):
            for l in range(2):
                ph_ffn_p1(l)
                barrier()
                ph_ffn_p2(l)
                barrier()
            ph_addback(x_fm)
            barrier()
            if stop_after == "attn":
                ph_attn()
                barrier()
                dump_stream_and_finish()
            else:
                ph_attn(spill_to=xbk_d)
                barrier()
                ph_ln0()
                barrier()
                for l in range(2):
                    ph_ffn_p1(l)
                    barrier()
                    ph_ffn_p2(l)
                    barrier()
                ph_addback(xbk_d, to_out=True)
        else:
            raise NotImplementedError(stop_after)

        with nc.Block() as block:
            @block.sync
            def _(e):
                plan.emit("sync", e)

            @block.gpsimd
            def _(e):
                plan.emit("gpsimd", e)

            @block.tensor
            def _(e):
                plan.emit("tensor", e)

            @block.vector
            def _(e):
                plan.emit("vector", e)

            @block.scalar
            def _(e):
                plan.emit("scalar", e)

    return nc



def _prep_shared(inp):
    f32 = np.float32
    sh = {}
    sh["ffn_wqkv"] = np.ascontiguousarray(inp["L0_ffn_wqkv"], dtype=f32)
    sh["ffn_wo"] = np.ascontiguousarray(inp["L0_ffn_wo"], dtype=f32)
    w1 = inp["L0_ffn_w1"]; b1 = inp["L0_ffn_b1"]; w2 = inp["L0_ffn_w2"]
    w1p = np.zeros((2, D, 2 * HIDP), f32)
    w1p[:, :, 0:HID] = w1[:, :, 0:HID]
    w1p[:, :, HIDP:HIDP + HID] = w1[:, :, HID:2 * HID]
    sh["ffn_w1p"] = w1p
    b1p = np.zeros((2, 2 * HIDP), f32)
    b1p[:, 0:HID] = b1[:, 0:HID]
    b1p[:, HIDP:HIDP + HID] = b1[:, HID:2 * HID]
    w2p = np.zeros((2, HIDP, D), f32)
    w2p[:, 0:HID, :] = w2
    sh["ffn_w2p"] = w2p
    fc = np.zeros((2, 128, M1 + 3 * C), f32)
    for l in range(2):
        fc[l, :, 0:M1] = b1p[l].reshape(M1, 128).T
        fc[l, :, M1:M1 + C] = inp["L0_ffn_b2"][l].reshape(C, 128).T
        fc[l, :, M1 + C:M1 + 2 * C] = inp["L0_ffn_ln_g"][l].reshape(C, 128).T
        fc[l, :, M1 + 2 * C:M1 + 3 * C] = inp["L0_ffn_ln_b"][l].reshape(C, 128).T
    sh["ffn_cols"] = fc
    qs = np.zeros((2, 128, 4), f32)
    for l in range(2):
        q_ = np.tile(inp["L0_ffn_q_scale"][l], 2)
        k_ = np.tile(inp["L0_ffn_k_scale"][l], 2)
        qsw = q_.reshape(4, 32)[[1, 0, 3, 2]].reshape(128)
        ksw = k_.reshape(4, 32)[[1, 0, 3, 2]].reshape(128)
        qs[l] = np.stack([q_, qsw, k_, ksw], axis=1)
    sh["ffn_qscol"] = qs
    sh["attn_wqkvuT"] = np.ascontiguousarray(inp["L0_attn_Wqkvu"].T, dtype=f32)
    sh["attn_woutT"] = np.ascontiguousarray(inp["L0_attn_Wout"].T, dtype=f32)
    b = inp["L0_attn_bqkvu"]
    ac = np.zeros((128, 4 * C), f32)
    ac[:, 0:C] = b[0:D].reshape(C, 128).T
    ac[:, C:2 * C] = b[3 * D:4 * D].reshape(C, 128).T
    ac[:, 2 * C:3 * C] = inp["L0_attn_bout"].reshape(C, 128).T
    ac[:, 3 * C:4 * C] = inp["L0_attn_ln_w"].reshape(C, 128).T
    sh["attn_cols"] = ac
    sh["attn_lnb"] = np.ascontiguousarray(inp["L0_attn_ln_b"].reshape(C, 128).T, dtype=f32)
    sh["attn_bkv_row"] = np.ascontiguousarray(b[D:3 * D].reshape(1, 2 * D), dtype=f32)
    sh["attn_theta_row"] = np.ascontiguousarray(np.asarray(inp["L0_attn_theta"], f32).reshape(1, D))
    sh["idx_row"] = np.arange(N, dtype=f32).reshape(1, N)

    t = np.arange(N, dtype=np.float64)
    half = np.arange(0, 64, 2, dtype=np.float64)
    invf = 1.0 / 10000.0 ** (half / 64.0)
    freqs = t[:, None] * invf[None, :]
    base = (half + 0.4 * 64) / (1.4 * 64)
    power = (t - N // 2) / 256.0
    scale = base[None, :] ** power[:, None]
    cos = np.cos(freqs); sin = np.sin(freqs)
    rows32 = lambda a: a.T
    csq = rows32(cos * scale) * 8.0
    snq = rows32(sin * scale) * 8.0
    csk = rows32(cos / scale)
    snk = rows32(sin / scale)
    sgn = np.concatenate([-np.ones((32, 1)), np.ones((32, 1))], axis=1)
    rot = np.zeros((4, 128, N), f32)
    for g in range(4):
        s = -1.0 if (g % 2 == 0) else 1.0
        rot[0, g * 32:(g + 1) * 32] = csq
        rot[1, g * 32:(g + 1) * 32] = s * snq
        rot[2, g * 32:(g + 1) * 32] = csk
        rot[3, g * 32:(g + 1) * 32] = s * snk
    sh["rot_tabs"] = rot

    p64 = np.zeros((128, 128), f32)
    for m in range(128):
        blk, off = divmod(m, 64)
        p64[blk * 64 + (off + 32) % 64, m] = 1.0
    sh["p64_swap"] = p64

    a = np.arange(128)
    m0 = np.where(a[:, None] >= a[None, :], 0.0, -1e30).astype(f32)
    m4 = np.where(a[:, None] <= a[None, :], 0.0, -1e30).astype(f32)
    sh["tri_masks"] = np.stack([m0, m4])

    selh = np.zeros((8, 8 * 128), f32)
    for h in range(8):
        selh[h, h * 128:(h + 1) * 128] = 1.0
    sh["sel_h"] = selh
    selblk = np.zeros((8, 128, 104), f32)
    for c in range(8):
        for p in range(128):
            hrow = 2 * (c % 4) + p // 64 + (64 if c >= 4 else 0)
            selblk[c, p, hrow] = 1.0
            selblk[c, p, hrow + 32] = 1.0
    sh["sel_blk"] = selblk
    selrep = np.zeros((4, 8, 128), f32)
    for c in range(4):
        selrep[c, 2 * c, 0:64] = 1.0
        selrep[c, 2 * c + 1, 64:128] = 1.0
    sh["sel_rep"] = selrep
    ones = np.zeros((6, 128), f32)
    ones[0] = 1.0
    ones[1] = EPS
    ones[2] = 1.0
    ones[3] = np.pi / 2
    ones[4] = 1e-30
    sh["ones_misc"] = ones
    sh["ones_row"] = np.ones((1, 128), f32)
    return sh


def kernel(**inp):
    stop_after = inp.pop("_stop_after", "full")
    x = np.asarray(inp["L0_attn_x"], dtype=np.float32)
    sh = _prep_shared(inp)
    nc = build(stop_after)
    in_maps = []
    for b in range(8):
        m = dict(sh)
        m["x_fm"] = np.ascontiguousarray(x[b].T)
        in_maps.append(m)
    res = run_bass_kernel_spmd(nc, in_maps, core_ids=list(range(8)))
    out = np.stack([np.asarray(r["out_fm"]).T for r in res.results], axis=0)
    return out.astype(np.float32)
